# version 30
# baseline (speedup 1.0000x reference)
import contextlib

import numpy as np
import ml_dtypes

import concourse.bass as bass
import concourse.mybir as mybir
from concourse.bass_utils import run_bass_kernel_spmd

F32 = mybir.dt.float32
BF16 = mybir.dt.bfloat16
I32 = mybir.dt.int32
AF = mybir.ActivationFunctionType
ALU = mybir.AluOpType
AX = mybir.AxisListType

f32 = np.float32

NCORES = 8
SEQ = 4
T = 2048
TILE = 512
NTT = T // TILE
TOK = SEQ * T
D = 1024
NCH = D // 128
DFF = 4096
NFC = DFF // 128
HID = 3072
NHC = HID // 128

DN = 64.0 ** -0.25
EPS_K = 1e-6
EPS_LN_OUT = 1e-5
EPS_LN_IN = 1e-6
INV127 = 1.0 / 127.0
DOWN_SCALE = 1.0 / (2.0 * 127.0 ** 3)

DEBUG = False
DBG_OC = 0
W1_MODE = "full"
PROBE_W = False

ENGS = ("pe", "ac", "ve", "po", "sp")
DMA_SEMS = (["scst", "sres", "sout", "sccd", "sdbg"]
            + [f"sx{i}" for i in range(8)]
            + [f"sstg{i}" for i in range(4)]
            + [f"sws{i}" for i in range(6)]
            + [f"sscr_{w}" for w in ("wq", "wo", "w1", "w2", "wg", "wu", "wd")])
SEMS = ("pe", "ac", "ve", "po", "cc") + tuple(DMA_SEMS)
DEF_SEM = {"pe": "pe", "ac": "ac", "ve": "ve", "po": "po"}


class Res:
    __slots__ = ("w", "readers", "name")

    def __init__(self, name=""):
        self.w = None
        self.readers = []
        self.name = name


class Sched:
    def __init__(self):
        self.ops = {e: [] for e in ENGS}
        self.counts = {s: 0 for s in SEMS}
        self.waited = {e: {} for e in ENGS}
        self.pe_n = 0
        self.marks = []

    def mark(self, name):
        self.marks.append((name, self.pe_n))

    def op(self, eng, fn, reads=(), writes=(), deps=(), sem=None, inc=None):
        sem = sem or DEF_SEM[eng]
        if inc is None:
            inc = 16 if sem not in DEF_SEM.values() else 1
        waits = {}

        def add(tok):
            if tok is None:
                return
            s, v = tok
            if s == "pe" and eng == "pe":
                return
            if v > waits.get(s, 0):
                waits[s] = v

        for r in reads:
            add(r.w)
        for w in writes:
            add(w.w)
            for tk in w.readers:
                add(tk)
        for d in deps:
            add(d)
        ew = self.waited[eng]
        final = [(s, v) for s, v in waits.items() if v > ew.get(s, 0)]
        for s, v in final:
            ew[s] = v
        self.counts[sem] += inc
        tok = (sem, self.counts[sem])
        self.ops[eng].append((final, fn, sem, inc))
        for r in reads:
            r.readers.append(tok)
        for w in writes:
            w.w = tok
            w.readers = []
        return tok



def build_program(debug=False, phases="full", c_stage=4, c_tiles=None):
    nc = bass.Bass("TRN2", target_bir_lowering=False, debug=False, num_devices=NCORES)
    S = Sched()

    def din(name, shape, dt):
        return nc.dram_tensor(name, shape, dt, kind="ExternalInput").ap()

    xfm = din("xfm", [D, TOK], F32)
    wq_h = din("wqt", [NCH, 128, NCH, 128], F32)
    wk_h = din("wk", [D, D], F32)
    wv_h = din("wv", [D, D], F32)
    wo_h = din("wot", [NCH, 128, NCH, 128], F32)
    projT_h = din("projT", [64, 64], F32)
    w1_h = din("w1t", [NFC, 128, NCH, 128], F32)
    w2_h = din("w2", [DFF, D], F32)
    b1_h = din("b1c", [128, NFC], F32)
    b2_h = din("b2c", [128, NCH], F32)
    g1_h = din("g1c", [128, NCH], F32)
    be1_h = din("be1c", [128, NCH], F32)
    g2_h = din("g2c", [128, NCH], F32)
    be2_h = din("be2c", [128, NCH], F32)
    wg_h = din("wgt", [NHC, 128, NCH, 128], I32)
    wu_h = din("wut", [NHC, 128, NCH, 128], I32)
    wd_h = din("wdt", [HID, D], I32)
    identb_h = din("identb", [128, 128], BF16)
    identf_h = din("identf", [128, 128], F32)
    onescol_f_h = din("onescol_f", [128, 1], F32)
    onescol_b_h = din("onescol_b", [128, 1], BF16)
    onesrow_f_h = din("onesrow_f", [1, 128], F32)
    onesblk_h = din("onesblk", [128, 2], BF16)
    sel01b_h = din("sel01b", [2, 128], BF16)
    sel01f_h = din("sel01f", [2, 128], F32)

    outfm = nc.dram_tensor("outfm", [D, TOK], F32, kind="ExternalOutput").ap()

    wqscr = nc.dram_tensor("wqscr", [NCH, 128, NCH, 128], BF16)
    woscr = nc.dram_tensor("woscr", [NCH, 128, NCH, 128], BF16)
    w1scr = nc.dram_tensor("w1scr", [NFC, 128, NCH, 128], BF16)
    w2scr = nc.dram_tensor("w2scr", [DFF, D], BF16)
    wgscr = nc.dram_tensor("wgscr", [NHC, 128, NCH, 128], BF16)
    wuscr = nc.dram_tensor("wuscr", [NHC, 128, NCH, 128], BF16)
    wdscr = nc.dram_tensor("wdscr", [HID, D], BF16)
    ccbuf = nc.dram_tensor("ccbuf", [1, 16384], F32)

    sb = nc.alloc_sbuf_tensor
    wkr = sb("wkr", [128, NCH, 1024], BF16)
    wvr = sb("wvr", [128, NCH, 1024], BF16)
    projbd = sb("projbd", [128, 128], BF16)
    identb = sb("identb_s", [128, 128], BF16)
    identf = sb("identf_s", [128, 128], F32)
    onescol_f = sb("onescol_f_s", [128, 1], F32)
    onescol_b = sb("onescol_b_s", [128, 1], BF16)
    onesrow_f = sb("onesrow_f_s", [1, 128], F32)
    onesblk = sb("onesblk_s", [128, 2], BF16)
    sel01b = sb("sel01b_s", [2, 128], BF16)
    sel01f = sb("sel01f_s", [2, 128], F32)
    b1c = sb("b1c_s", [128, NFC], F32)
    b2c = sb("b2c_s", [128, NCH], F32)
    b2eff = sb("b2eff", [128, NCH], F32)
    g1c = sb("g1c_s", [128, NCH], F32)
    be1c = sb("be1c_s", [128, NCH], F32)
    g2c = sb("g2c_s", [128, NCH], F32)
    be2c = sb("be2c_s", [128, NCH], F32)
    stgproj = sb("stgproj", [64, 64], F32)

    x_sb = sb("x_sb", [128, NCH, TILE], F32)
    xn_sb2 = [sb(f"xn_sb{i}", [128, NCH, TILE], BF16) for i in range(2)]
    res_sb = sb("res_sb", [128, NCH, TILE], F32)
    out1b = sb("out1b", [128, NCH, TILE], BF16)
    h1_sb = sb("h1_sb", [128, NFC, TILE], BF16)
    hg_sb = sb("hg_sb", [128, NHC, TILE], BF16)
    attn_sb = sb("attn_sb", [128, NCH, TILE], BF16)
    gscr = [sb(f"gscr{i}", [128, TILE], BF16) for i in range(2)]
    uscr = [sb(f"uscr{i}", [128, TILE], BF16) for i in range(2)]
    gtan = [sb(f"gtan{i}", [128, TILE], BF16) for i in range(2)]
    escr = [sb(f"escr{i}", [128, TILE], BF16) for i in range(2)]
    rscr = [sb(f"rscr{i}", [128, TILE], BF16) for i in range(2)]

    kvq = [sb(f"kvq{s}", [128, 1024], BF16) for s in range(SEQ)]
    denpat = [sb(f"denpat{s}", [128, 16], BF16) for s in range(SEQ)]
    ksum = [sb(f"ksum{s}", [128, NCH], F32) for s in range(SEQ)]
    vsum = [sb(f"vsum{s}", [128, NCH], F32) for s in range(SEQ)]
    mxrun = sb("mxrun", [128, 1], F32)
    mxg = sb("mxg", [1, 1], F32)
    emx1 = sb("emx1", [1, 1], F32)
    emx_col = sb("emx_col", [128, 1], F32)
    zero_sb = sb("zero_sb", [128, 128], F32)

    kbf = [h1_sb[:, 0, :], h1_sb[:, 1, :]]
    vbf = [h1_sb[:, 2, :], h1_sb[:, 3, :]]
    kfb = [uscr[0][:], uscr[1][:]]
    kfT = [gtan[0][:], gtan[1][:]]
    vT = [escr[0][:], escr[1][:]]
    k2b = [out1b[:, 2, :], out1b[:, 3, :]]
    negd = [h1_sb[0:2, 16, :], h1_sb[0:2, 17, :]]
    negmx = [h1_sb[0:2, 18, :], h1_sb[0:2, 19, :]]
    d1r = [h1_sb[0:2, 20, :], h1_sb[0:2, 21, :]]
    denf = d1r
    sesb = [h1_sb[0:2, 22, :], h1_sb[0:2, 23, :]]
    mxt = [sb(f"mxt{i}", [128, 8], BF16) for i in range(2)]
    qbf_all = [h1_sb[:, i, :] for i in range(8)]
    ddb_all = [h1_sb[:, 8 + i, :] for i in range(8)]
    kbf_all = qbf_all
    vbf_all = ddb_all
    qf_sb = [out1b[:, 0, :], out1b[:, 1, :]]
    k2a = [out1b[:, i, :] for i in range(8)]

    h1f32 = h1_sb.bitcast(F32)
    o1f32 = out1b.bitcast(F32)
    t1f = [o1f32[:, 4:6, :].rearrange("p a b -> p (a b)"),
           o1f32[:, 6:8, :].rearrange("p a b -> p (a b)")]
    recip = [h1f32[0:2, 24:26, :].rearrange("p a b -> p (a b)"),
             h1f32[0:2, 26:28, :].rearrange("p a b -> p (a b)")]
    tscr = [h1f32[:, 28:30, :].rearrange("p a b -> p (a b)"),
            h1f32[:, 30:32, :].rearrange("p a b -> p (a b)")]
    x2scr = [h1_sb[:, 12, :], h1_sb[:, 13, :]]
    ksum_t = [sb(f"ksum_t{i}", [128, 1], F32) for i in range(2)]
    vsum_t = [sb(f"vsum_t{i}", [128, 1], F32) for i in range(2)]
    mxtt = [sb(f"mxtt{i}", [128, 1], F32) for i in range(2)]
    murow = sb("murow_t", [1, TILE], F32)[:]
    musq = sb("musq_t", [1, TILE], F32)[:]
    varrow = sb("varrow_t", [1, TILE], F32)[:]
    rstd = sb("rstd_t", [1, TILE], F32)[:]

    NSLOT = 6
    wslot = [sb(f"wslot{i}", [128, 2048], BF16) for i in range(NSLOT)]
    stg = [wslot[i][:, 0:1024] for i in range(4)]

    pA = nc.alloc_psum_tensor("pA", [128, 1024], F32)
    pB = nc.alloc_psum_tensor("pB", [128, 1024], F32)
    pC = nc.alloc_psum_tensor("pC", [128, 1024], F32)
    pD = nc.alloc_psum_tensor("pD", [128, 512], F32)
    vTp = nc.alloc_psum_tensor("vTp", [128, 1024], BF16)
    vTf = vTp.bitcast(F32)
    pDbf = pD.bitcast(BF16)
    pCbf = pC.bitcast(BF16)

    A0 = lambda: pA[:, 0:512]
    A1 = lambda: pA[:, 512:1024]
    B0 = lambda: pB[:, 0:512]
    B1 = lambda: pB[:, 512:1024]
    C0 = lambda: pC[:, 0:512]
    C1 = lambda: pC[:, 512:1024]
    D1 = lambda: pD[:, 0:512]
    BANK = [A0, A1, B0, B1, C0, C1, lambda: vTf[:, 0:512], D1]

    rA0, rA1, rB0, rB1, rC0, rC1, rD0, rD1 = (Res(n) for n in
        ("A0", "A1", "B0", "B1", "C0", "C1", "D0", "D1"))
    RBANK = [rA0, rA1, rB0, rB1, rC0, rC1, rD0, rD1]

    r_x, r_res, r_out1b, r_attn, r_h1, r_hg = (Res(n) for n in
        ("x", "res", "out1b", "attn", "h1", "hg"))
    r_xc = [Res(f"xc{c}") for c in range(NCH)]
    r_xn2 = [[Res(f"xn0_{c}") for c in range(NCH)],
             [Res(f"xn1_{c}") for c in range(NCH)]]
    r_kvq = [Res(f"kvq{s}") for s in range(SEQ)]
    r_denpat = [Res(f"denpat{s}") for s in range(SEQ)]
    r_ksum = [Res(f"ksum{s}") for s in range(SEQ)]
    r_vsum = [Res(f"vsum{s}") for s in range(SEQ)]
    r_mxrun = Res("mxrun")
    r_mx = Res("mx")
    r_emx = Res("emx")
    r_wslot = [Res(f"wslot{i}") for i in range(NSLOT)]
    RW = (lambda i: []) if PROBE_W else (lambda i: [r_wslot[i]])
    r_stg = r_wslot
    two = lambda n: [Res(n + "0"), Res(n + "1")]
    r_kbf, r_vbf, r_k2, r_kfb, r_kfT, r_vT = (two(n) for n in
        ("kbf", "vbf", "k2", "kfb", "kfT", "vT"))
    r_mxt, r_qf = two("mxt"), two("qf")
    r_k2a = [Res(f"k2a{i}") for i in range(8)]
    r_attn_c = [Res(f"attnc{i}") for i in range(8)]
    r_qbf_all = [Res(f"qbf{i}") for i in range(8)]
    r_ddb_all = [Res(f"ddbA{i}") for i in range(8)]
    r_kbf_all = r_qbf_all
    r_vbf_all = r_ddb_all
    r_negd = two("negd")
    r_negmx = two("negmx")
    r_sesb, r_d1, r_recip = (two(n) for n in ("sesb", "d1", "recip"))
    r_denf = r_d1
    r_t1 = two("t1")
    r_tscr, r_x2, r_escr, r_rscr, r_ksumt, r_vsumt, r_mxtt = (two(n) for n in
        ("tscr", "x2", "escr", "rscr", "ksumt", "vsumt", "mxtt"))
    r_gscr, r_uscr, r_gtan = two("gscr"), two("uscr"), two("gtan")
    r_kfb, r_kfT, r_vT = r_uscr, r_gtan, r_escr
    r_rows = Res("rows")
    r_consts = Res("consts")
    r_b2eff = Res("b2eff")
    r_projbd = Res("projbd")
    r_wres = Res("wres")
    r_scr = {"w1": Res("w1scr"), "w2": Res("w2scr"), "wg": Res("wgscr"),
             "wu": Res("wuscr"), "wd": Res("wdscr"), "wq": Res("wqscr"),
             "wo": Res("woscr")}
    r_cc = Res("ccbuf")

    if debug:
        dbg_xn = nc.dram_tensor("dbg_xn", [128, NCH, TILE], BF16, kind="ExternalOutput").ap()
        dbg_kv = nc.dram_tensor("dbg_kv", [128, 1024], BF16, kind="ExternalOutput").ap()
        dbg_ks = nc.dram_tensor("dbg_ks", [128, NCH], F32, kind="ExternalOutput").ap()
        dbg_vs = nc.dram_tensor("dbg_vs", [128, NCH], F32, kind="ExternalOutput").ap()
        dbg_mx = nc.dram_tensor("dbg_mx", [1, 1], F32, kind="ExternalOutput").ap()
        dbg_attn = nc.dram_tensor("dbg_attn", [128, NCH, TILE], BF16, kind="ExternalOutput").ap()
        dbg_out1 = nc.dram_tensor("dbg_out1", [128, NCH, TILE], BF16, kind="ExternalOutput").ap()
        dbg_h1 = nc.dram_tensor("dbg_h1", [128, 4, TILE], BF16, kind="ExternalOutput").ap()
        dbg_a = nc.dram_tensor("dbg_a", [128, NCH, TILE], BF16, kind="ExternalOutput").ap()
        dbg_qf = nc.dram_tensor("dbg_qf", [128, TILE], BF16, kind="ExternalOutput").ap()
        dbg_den = nc.dram_tensor("dbg_den", [2, TILE], F32, kind="ExternalOutput").ap()
        dbg_d1 = nc.dram_tensor("dbg_d1", [2, TILE], F32, kind="ExternalOutput").ap()
        dbg_nmx = nc.dram_tensor("dbg_nmx", [2, TILE], F32, kind="ExternalOutput").ap()
        dbg_ngd = nc.dram_tensor("dbg_ngd", [2, TILE], F32, kind="ExternalOutput").ap()
        dbg_t1 = nc.dram_tensor("dbg_t1", [128, TILE], F32, kind="ExternalOutput").ap()
        dbg_qb = nc.dram_tensor("dbg_qb", [128, TILE], BF16, kind="ExternalOutput").ap()
        dbg_wq = nc.dram_tensor("dbg_wq", [128, 2048], BF16, kind="ExternalOutput").ap()

    def dma(q, out_ap, in_ap, sem, reads=(), writes=(), deps=(), slow=False):
        kw = {"allow_slow_non_contiguous": True} if slow else {}
        return S.op(q, lambda e, o=out_ap, i=in_ap, k=kw: e.dma_start(out=o, in_=i, **k),
                    reads=reads, writes=writes, deps=deps, sem=sem, inc=16)

    def mmg(items, reads=(), writes=(), deps=()):
        S.pe_n += len(items)
        def fn(e, items=items):
            last = None
            for it in items:
                o, l, r, st, sp = it[:5]
                kw = dict(start=st, stop=sp, skip_group_check=True)
                if len(it) > 5:
                    kw["tile_position"] = it[5]
                last = e.matmul(o, l, r, **kw)
            return last
        return S.op("pe", fn, reads=reads, writes=writes, deps=deps)

    def tpose(out_ap, in_ap, ident, reads=(), writes=(), deps=()):
        S.pe_n += 1
        return S.op("pe", lambda e, o=out_ap, i=in_ap, d=ident: e.transpose(o, i, d),
                    reads=reads, writes=writes, deps=deps)

    def tposeg(items, reads=(), writes=(), deps=()):
        S.pe_n += len(items)
        def fn(e, items=items):
            last = None
            for o, i, d in items:
                last = e.transpose(o, i, d)
            return last
        return S.op("pe", fn, reads=reads, writes=writes, deps=deps)

    def veg(calls, reads=(), writes=(), deps=()):
        def fn(e, calls=calls):
            last = None
            for m, a, k in calls:
                last = getattr(e, m)(*a, **k)
            return last
        return S.op("ve", fn, reads=reads, writes=writes, deps=deps)

    def act(out_ap, in_ap, func, reads=(), writes=(), deps=(), **kw):
        return S.op("ac",
                    lambda e, o=out_ap, i=in_ap, fu=func, k=dict(kw):
                        e.activation(out=o, in_=i, func=fu, **k),
                    reads=reads, writes=writes, deps=deps)

    def ve(meth, *args, reads=(), writes=(), deps=(), **kw):
        return S.op("ve", lambda e, m=meth, a=args, k=dict(kw): getattr(e, m)(*a, **k),
                    reads=reads, writes=writes, deps=deps)

    def po(meth, *args, reads=(), writes=(), deps=(), **kw):
        return S.op("po", lambda e, m=meth, a=args, k=dict(kw): getattr(e, m)(*a, **k),
                    reads=reads, writes=writes, deps=deps, sem="po", inc=1)

    for dst, src in [
        (identb[:], identb_h), (identf[:], identf_h),
        (onescol_f[:], onescol_f_h), (onescol_b[:], onescol_b_h),
        (onesrow_f[:], onesrow_f_h), (onesblk[:], onesblk_h),
        (sel01b[:], sel01b_h), (sel01f[:], sel01f_h),
        (b1c[:], b1_h), (b2c[:], b2_h),
        (g1c[:], g1_h), (be1c[:], be1_h), (g2c[:], g2_h), (be2c[:], be2_h),
        (stgproj[:], projT_h),
    ]:
        dma("sp", dst, src, "scst", writes=[r_consts])

    for kc in range(NCH):
        dma("po", wkr[:, kc, :], wk_h[kc * 128:(kc + 1) * 128, :], "sres", writes=[r_wres])
    for kc in range(NCH):
        dma("po", wvr[:, kc, :], wv_h[kc * 128:(kc + 1) * 128, :], "sres", writes=[r_wres])
    wkwv_done = r_wres.w

    po("memset", projbd[:], 0.0, writes=[r_projbd])
    ve("tensor_scalar", projbd[0:64, 0:64], stgproj[:], float(DN), None, ALU.mult,
       reads=[r_consts], writes=[r_projbd])
    ve("tensor_scalar", projbd[64:128, 64:128], stgproj[:], float(DN), None, ALU.mult,
       reads=[r_consts], writes=[r_projbd])

    po("memset", zero_sb[:], 0.0, writes=[r_cc])
    dma("po", ccbuf.ap().rearrange("o (p f) -> p (o f)", p=128), zero_sb[:],
        "sccd", reads=[r_cc], writes=[r_cc])
    for s in range(SEQ):
        po("memset", kvq[s][:], 0.0, writes=[r_kvq[s]])
        po("memset", denpat[s][:], 0.0, writes=[r_denpat[s]])
        po("memset", ksum[s][:], 0.0, writes=[r_ksum[s]])
        po("memset", vsum[s][:], 0.0, writes=[r_vsum[s]])
    ve("memset", mxrun[:], -1e30, writes=[r_mxrun])

    def emit_xtile(tile):
        t0 = tile * TILE
        xv = xfm[:, t0:t0 + TILE].rearrange("(c p) n -> p c n", p=128)
        for c in range(NCH):
            dma("ac", x_sb[:, c, :], xv[:, c, :],
                f"sx{c}", writes=[r_xc[c]])

    def emit_stats(src_sb, r_src, eps, pS, rS, fill=None, x2b=None):
        x2t, r_x2t = x2b if x2b is not None else (x2scr, r_x2)
        for c in range(NCH):
            rs = r_src[c] if isinstance(r_src, list) else r_src
            act(x2t[c % 2], src_sb[:, c, :], AF.Square,
                reads=[rs], writes=[r_x2t[c % 2]])
            mmg([(pS()[0:1, :], onescol_f[:], src_sb[:, c, :],
                  c == 0, c == NCH - 1),
                 (pS()[32:33, :], onescol_b[:], x2t[c % 2],
                  c == 0, c == NCH - 1, (0, 32))],
                reads=[rs, r_x2t[c % 2], r_consts], writes=[rS])
        ve("tensor_scalar", murow, pS()[0:1, :], 1.0 / D, None, ALU.mult,
           reads=[rS], writes=[r_rows])
        ve("tensor_mul", musq, murow, murow,
           reads=[r_rows], writes=[r_rows])
        ve("scalar_tensor_tensor", varrow, pS()[32:33, :], 1.0 / D,
           musq, ALU.mult, ALU.subtract,
           reads=[rS, r_rows], writes=[r_rows])
        if fill is not None:
            fill()
        act(rstd, varrow, AF.Ln, bias=f32(eps),
            reads=[r_rows], writes=[r_rows])
        act(rstd, rstd, AF.Exp, scale=f32(-0.5),
            reads=[r_rows], writes=[r_rows])

    def emit_xn(pS, rS, pX, rX, pY, rY, x2b=None):
        emit_stats(x_sb, r_xc, EPS_LN_OUT, pS, rS, x2b=x2b)
        mmg([(pX(), onesrow_f[:], rstd, True, True),
             (pY(), onesrow_f[:], murow, True, True)],
            reads=[r_rows, r_consts], writes=[rX, rY])
        for c in range(NCH):
            ve("tensor_sub", tscr[c % 2], x_sb[:, c, :], pY(),
               reads=[r_xc[c], rY], writes=[r_tscr[c % 2]])
            ve("tensor_mul", xn_t[:, c, :], tscr[c % 2], pX(),
               reads=[r_tscr[c % 2], rX], writes=[r_xn_t[c]])

    def emit_affine_ln(src_sb, r_src, dst, r_dst, g_col, b_col, eps,
                       pS, rS, pX, rX, pY, rY, fill=None):
        emit_stats(src_sb, r_src, eps, pS, rS, fill=fill)
        mmg([(pX(), onesrow_f[:], rstd, True, True),
             (pY(), onesrow_f[:], murow, True, True)],
            reads=[r_rows, r_consts], writes=[rX, rY])
        for c in range(NCH):
            ve("tensor_sub", tscr[c % 2], src_sb[:, c, :], pY(),
               reads=[r_src, rY], writes=[r_tscr[c % 2]])
            ve("tensor_mul", tscr[c % 2], tscr[c % 2], pX(),
               reads=[r_tscr[c % 2], rX], writes=[r_tscr[c % 2]])
            ve("tensor_scalar", dst[:, c, :], tscr[c % 2],
               g_col[:, c:c + 1], b_col[:, c:c + 1], ALU.mult, ALU.add,
               reads=[r_tscr[c % 2], r_consts], writes=[r_dst])

    for s in range(SEQ):
        for t in range(NTT):
            tile = s * NTT + t
            xn_t = xn_sb2[tile % 2]
            r_xn_t = r_xn2[tile % 2]
            S.mark(f"AB{tile}")
            if tile == 0:
                emit_xtile(0)
            emit_xn(C0, rC0, C1, rC1, D1, rD1,
                    x2b=([gscr[0][:], gscr[1][:]], r_gscr))
            for oc in range(NCH):
                par = oc % 2
                pK, rK = [(B0(), rB0), (B1(), rB1),
                           (BANK[6](), rD0)][oc % 3]
                pV, rV = [(C0(), rC0), (C1(), rC1),
                          (D1(), rD1)][oc % 3]
                if oc == 0:
                    for kc in range(NCH):
                        mmg([(pK, wkr[:, kc, oc * 128:(oc + 1) * 128],
                              xn_t[:, kc, :], kc == 0, kc == NCH - 1)],
                            reads=[r_xn_t[kc], r_wres], writes=[rK],
                            deps=[wkwv_done])
                        mmg([(pV, wvr[:, kc, oc * 128:(oc + 1) * 128],
                              xn_t[:, kc, :], kc == 0, kc == NCH - 1)],
                            reads=[r_xn_t[kc], r_wres], writes=[rV],
                            deps=[wkwv_done])
                else:
                    mmg([(pK, wkr[:, kc, oc * 128:(oc + 1) * 128],
                          xn_t[:, kc, :], kc == 0, kc == NCH - 1)
                         for kc in range(NCH)],
                        reads=list(r_xn_t) + [r_wres], writes=[rK],
                        deps=[wkwv_done])
                    mmg([(pV, wvr[:, kc, oc * 128:(oc + 1) * 128],
                          xn_t[:, kc, :], kc == 0, kc == NCH - 1)
                         for kc in range(NCH)],
                        reads=list(r_xn_t) + [r_wres], writes=[rV],
                        deps=[wkwv_done])
                act(kbf_all[oc], pK, AF.Copy,
                    reads=[rK], writes=[r_kbf_all[oc]])
                act(vbf_all[oc], pV, AF.Copy, accum_out=vsum_t[par][:],
                    reads=[rV], writes=[r_vbf_all[oc], r_vsumt[par]])
                ve("scalar_tensor_tensor", vsum[s][:, oc:oc + 1], vsum_t[par][:],
                   float(EPS_K), vsum[s][:, oc:oc + 1], ALU.mult, ALU.add,
                   reads=[r_vsumt[par]], writes=[r_vsum[s]])
            emit_xtile((tile + 1) % (SEQ * NTT))
            for oc in range(NCH):
                ve("tensor_mul", k2a[oc], kbf_all[oc], kbf_all[oc],
                   reads=[r_kbf_all[oc]], writes=[r_k2a[oc]])

            def ab2a(oc):
                par = oc % 2
                pDD, rDD = (B0(), rB0) if par == 0 else (B1(), rB1)
                mmg([(pDD, projbd[:], kbf_all[oc], True, True),
                     (C0()[0:2, :], onesblk[:], k2a[oc], True, True)],
                    reads=[r_kbf_all[oc], r_projbd, r_k2a[oc], r_consts],
                    writes=[rDD, rC0])
                ve("reduce_max", mxtt[par][:], pDD, reads=[rDD],
                   writes=[r_mxtt[par]], axis=AX.X)
                ve("tensor_max", mxrun[:], mxrun[:], mxtt[par][:],
                   reads=[r_mxtt[par]], writes=[r_mxrun])
                ve("tensor_scalar", negd[par], C0()[0:2, :], -1.0 / 16.0,
                   None, ALU.mult, reads=[rC0], writes=[r_negd[par]])

            def ab2b(oc):
                par = oc % 2
                pDD, rDD = (B0(), rB0) if par == 0 else (B1(), rB1)
                mmg([(pDD, sel01b[:], negd[par], False, True)],
                    reads=[r_negd[par], r_consts], writes=[rDD])
                act(kfb[par], pDD, AF.Exp, accum_out=ksum_t[par][:],
                    reads=[rDD], writes=[r_kfb[par], r_ksumt[par]])
                ve("tensor_add", ksum[s][:, oc:oc + 1], ksum[s][:, oc:oc + 1],
                   ksum_t[par][:], reads=[r_ksumt[par]], writes=[r_ksum[s]])

            def ab3(oc):
                par = oc % 2
                tposeg([(vTp[:, j * 128:(j + 1) * 128],
                         kfb[par][:, j * 128:(j + 1) * 128], identb[:])
                        for j in range(4)],
                       reads=[r_kfb[par], r_consts], writes=[rD0])
                act(kfT[par], vTp[:, 0:512], AF.Copy,
                    reads=[rD0], writes=[r_kfT[par]])
                tposeg([(pDbf[:, j * 128:(j + 1) * 128],
                         vbf_all[oc][:, j * 128:(j + 1) * 128], identb[:])
                        for j in range(4)],
                       reads=[r_vbf_all[oc], r_consts], writes=[rD1])
                ve("tensor_copy", vT[par], pDbf[:, 0:512],
                   reads=[rD1], writes=[r_vT[par]])

            def ab4(oc):
                par = oc % 2
                kv_first = (t == 0 and oc in (0, 4))
                rKV = rA0 if oc < 4 else rA1
                mmg([(pA[:, oc * 128:(oc + 1) * 128],
                      kfT[par][:, j * 128:(j + 1) * 128],
                      vT[par][:, j * 128:(j + 1) * 128],
                      kv_first and j == 0, True) for j in range(4)],
                    reads=[r_kfT[par], r_vT[par]], writes=[rKV])

            for step in range(NCH + 2):
                if step < NCH:
                    ab2a(step)
                if 1 <= step <= NCH:
                    ab3(step - 1)
                if step < NCH:
                    ab2b(step)
                if step >= 2:
                    ab4(step - 2)
        for oc in range(NCH):
            rKV = rA0 if oc < 4 else rA1
            act(kvq[s][0:64, oc * 128:oc * 128 + 64],
                pA[0:64, oc * 128:oc * 128 + 64], AF.Copy,
                reads=[rKV], writes=[r_kvq[s]])
            act(kvq[s][64:128, oc * 128 + 64:oc * 128 + 128],
                pA[64:128, oc * 128 + 64:oc * 128 + 128], AF.Copy,
                reads=[rKV], writes=[r_kvq[s]])
            ve("tensor_copy", denpat[s][0:64, 2 * oc:2 * oc + 1],
               ksum[s][0:64, oc:oc + 1], reads=[r_ksum[s]], writes=[r_denpat[s]])
            ve("tensor_copy", denpat[s][64:128, 2 * oc + 1:2 * oc + 2],
               ksum[s][64:128, oc:oc + 1], reads=[r_ksum[s]], writes=[r_denpat[s]])

    if debug:
        dma("po", dbg_kv, kvq[0][:], "sdbg", reads=[r_kvq[0]])
        dma("po", dbg_ks, ksum[0][:], "sdbg", reads=[r_ksum[0]])
        dma("po", dbg_vs, vsum[0][:], "sdbg", reads=[r_vsum[0]])

    def conv(src_ap, dst_ap, sl, which, extra=None):
        dma("po", stg[sl][:], src_ap, f"sstg{sl}", writes=[r_stg[sl]])
        if extra is not None:
            extra(sl)
        dma("sp", dst_ap, stg[sl][:], f"sscr_{which}",
            reads=[r_stg[sl]], writes=[r_scr[which]])

    for i in range(NCH):
        conv(wq_h[i].rearrange("p k j -> p (k j)"),
             wqscr.ap()[i].rearrange("p k j -> p (k j)"), i % 4, "wq")
    for i in range(NCH):
        conv(wo_h[i].rearrange("p k j -> p (k j)"),
             woscr.ap()[i].rearrange("p k j -> p (k j)"), i % 4, "wo")
    for i in range(NFC):
        conv(w1_h[i].rearrange("p k j -> p (k j)"),
             w1scr.ap()[i].rearrange("p k j -> p (k j)"), i % 4, "w1")
    mmg([(pB[:, 0:128], onesrow_f[:], identf[0:1, :], True, True)],
        reads=[r_consts], writes=[rB0])
    for i in range(NFC):
        def w2_extra(sl, i=i):
            mmg([(pB[:, c:c + 1], stg[sl][:, c * 128:(c + 1) * 128], onescol_b[:],
                  False, i == NFC - 1) for c in range(NCH)],
                reads=[r_stg[sl], r_consts], writes=[rB0])
        conv(w2_h[i * 128:(i + 1) * 128, :],
             w2scr.ap()[i * 128:(i + 1) * 128, :], i % 4, "w2", extra=w2_extra)
    for c in range(NCH):
        ve("tensor_sub", b2eff[:, c:c + 1], b2c[:, c:c + 1], pB[:, c:c + 1],
           reads=[rB0, r_consts], writes=[r_b2eff])
    for i in range(NHC):
        conv(wg_h[i].rearrange("p k j -> p (k j)"),
             wgscr.ap()[i].rearrange("p k j -> p (k j)"), i % 4, "wg")
    for i in range(NHC):
        conv(wu_h[i].rearrange("p k j -> p (k j)"),
             wuscr.ap()[i].rearrange("p k j -> p (k j)"), i % 4, "wu")
    for i in range(HID // 128):
        conv(wd_h[i * 128:(i + 1) * 128, :],
             wdscr.ap()[i * 128:(i + 1) * 128, :], i % 4, "wd")


    S.mark("ABend")
    tpose(B0()[0:1, 0:128], mxrun[:], identf[:],
          reads=[r_mxrun, r_consts], writes=[rB0])
    ve("reduce_max", mxg[0:1, 0:1], B0()[0:1, 0:128], reads=[rB0],
       writes=[r_mx], axis=AX.X)
    dma("po", ccbuf.ap()[0:1, 0:1], mxg[0:1, 0:1], "sccd", reads=[r_mx], writes=[r_cc])
    S.op("po", lambda e: e.collective_compute(
        "AllReduce", ALU.max, replica_groups=[list(range(NCORES))],
        ins=[ccbuf.ap()], outs=[ccbuf.ap()]),
        reads=[r_cc], writes=[r_cc], sem="cc", inc=1)
    dma("po", mxg[0:1, 0:1], ccbuf.ap()[0:1, 0:1], "sccd", reads=[r_cc], writes=[r_mx])
    if debug:
        dma("po", dbg_mx, mxg[0:1, 0:1], reads=[r_mx])
    act(emx1[0:1, 0:1], mxg[0:1, 0:1], AF.Exp, scale=f32(-1.0),
        reads=[r_mx], writes=[r_emx])
    mmg([(B0()[:, 0:1], onesrow_f[:], emx1[0:1, 0:1], True, True)],
        reads=[r_emx, r_consts], writes=[rB0])
    act(emx_col[:], B0()[:, 0:1], AF.Copy, reads=[rB0], writes=[r_emx])
    for s in range(SEQ):
        ve("tensor_scalar", kvq[s][:], kvq[s][:], emx_col[:, 0:1], None,
           ALU.mult, reads=[r_emx, r_kvq[s]], writes=[r_kvq[s]])
        ve("tensor_scalar", denpat[s][:], denpat[s][:], emx_col[:, 0:1],
           float(EPS_K * T), ALU.mult, ALU.add,
           reads=[r_emx, r_denpat[s]], writes=[r_denpat[s]])
        cvA = ccbuf.ap()[0:1, 2048 + s * 1024:2048 + s * 1024 + 512]
        cvB = ccbuf.ap()[0:1, 2048 + s * 1024 + 512:2048 + (s + 1) * 1024]
        dma("po", cvA.rearrange("x (p c) -> (x p) c", p=128), vsum[s][:, 0:4],
            "sccd", reads=[r_vsum[s]], writes=[r_cc])
        dma("po", cvB.rearrange("x (p c) -> (x p) c", p=128), vsum[s][:, 4:8],
            "sccd", reads=[r_vsum[s]], writes=[r_cc])
        dma("po", murow, cvA, "sccd", reads=[r_cc], writes=[r_rows])
        dma("po", musq, cvB, "sccd", reads=[r_cc], writes=[r_rows])
        mmg([(B0(), onesrow_f[:], murow.rearrange("x (p c) -> x c p", p=128),
              True, True),
             (B1(), onesrow_f[:], musq.rearrange("x (p c) -> x c p", p=128),
              True, True)],
            reads=[r_rows, r_consts], writes=[rB0, rB1])
        act(gscr[0][:], B0(), AF.Copy, reads=[rB0], writes=[r_gscr[0]])
        act(gscr[1][:], B1(), AF.Copy, reads=[rB1], writes=[r_gscr[1]])
        ve("tensor_add", kvq[s][:, 0:512], kvq[s][:, 0:512], gscr[0][:],
           reads=[r_gscr[0]], writes=[r_kvq[s]])
        ve("tensor_add", kvq[s][:, 512:1024], kvq[s][:, 512:1024], gscr[1][:],
           reads=[r_gscr[1]], writes=[r_kvq[s]])

    LOOKAHEAD = 4
    load_fns = []
    issued = []
    slot_ctr = [0]

    def _issue_next():
        i = slot_ctr[0] % NSLOT
        slot_ctr[0] += 1
        load_fns[len(issued)](i)
        issued.append(i)

    def take(idx):
        while len(issued) <= min(idx + LOOKAHEAD, len(load_fns) - 1):
            _issue_next()
        return issued[idx]

    def mk_blk2(scr_t, g, which):
        def f(i):
            dma("sp" if i % 2 == 0 else "po",
                wslot[i][:].rearrange("p (f kj) -> p f kj", f=2),
                scr_t.ap()[2 * g:2 * g + 2].rearrange("f p k j -> p f (k j)"),
                f"sws{i}", reads=[r_scr[which]], writes=[r_wslot[i]])
        return f

    def mk_rows2(scr_t, g, which):
        def f(i):
            dma("sp" if i % 2 == 0 else "po",
                wslot[i][:].rearrange("p (a d) -> p a d", a=2),
                scr_t.ap()[g * 256:(g + 1) * 256, :].rearrange("(a p) d -> p a d", p=128),
                f"sws{i}", reads=[r_scr[which]], writes=[r_wslot[i]])
        return f

    def tile_load_plan():
        plan = []
        for g in range(NCH // 2):
            plan.append(mk_blk2(wqscr, g, "wq"))
        for g in range(NCH // 2):
            plan.append(mk_blk2(woscr, g, "wo"))
        for g in range(NFC // 2):
            plan.append(mk_blk2(w1scr, g, "w1"))
        for g in range(NFC // 2):
            plan.append(mk_rows2(w2scr, g, "w2"))
        for g in range(NHC // 2):
            plan.append(mk_blk2(wgscr, g, "wg"))
            plan.append(mk_blk2(wuscr, g, "wu"))
        for g in range(HID // 256):
            plan.append(mk_rows2(wdscr, g, "wd"))
        return plan

    n_per_tile = None
    for _tile in range(SEQ * NTT):
        p = tile_load_plan()
        if n_per_tile is None:
            n_per_tile = len(p)
        load_fns.extend(p)

    if c_tiles is None:
        c_tiles = SEQ * NTT if phases == "full" else 0
    for s in range(SEQ):
        for t in range(NTT):
            tile = s * NTT + t
            if tile >= c_tiles:
                continue
            base = tile * n_per_tile
            xn_t = xn_sb2[tile % 2]
            r_xn_t = r_xn2[tile % 2]
            S.mark(f"C{tile}_xn")
            if tile == 0:
                emit_xn(B0, rB0, A0, rA0, A1, rA1)

            for oc in range(NCH):
                pQ, rQ = (A0(), rA0) if oc % 2 == 0 else (A1(), rA1)
                iq = take(base + oc // 2)
                wqv = wslot[iq][:].rearrange("p (f k j) -> p f k j", f=2, k=NCH)
                if oc == 0:
                    for kc in range(NCH):
                        mmg([(pQ, wqv[:, oc % 2, kc, :], xn_t[:, kc, :],
                              kc == 0, kc == NCH - 1)],
                            reads=[r_xn_t[kc]] + RW(iq), writes=[rQ])
                else:
                    mmg([(pQ, wqv[:, oc % 2, kc, :], xn_t[:, kc, :],
                          kc == 0, kc == NCH - 1) for kc in range(NCH)],
                        reads=list(r_xn_t) + RW(iq), writes=[rQ])
                act(qbf_all[oc], pQ, AF.Copy, reads=[rQ], writes=[r_qbf_all[oc]])

            def s2(oc):
                par = oc % 2
                pDD, rDD = (B1(), rB1) if par == 0 else (D1(), rD1)
                ve("tensor_mul", k2b[par], qbf_all[oc], qbf_all[oc],
                   reads=[r_qbf_all[oc]], writes=[r_k2[par]])
                mmg([(pDD, projbd[:], qbf_all[oc], True, True)],
                    reads=[r_qbf_all[oc], r_projbd], writes=[rDD])
                act(ddb_all[oc], pDD, AF.Copy,
                    reads=[rDD], writes=[r_ddb_all[oc]])
                mmg([(B0()[0:2, :], onesblk[:], k2b[par], True, True)],
                    reads=[r_k2[par], r_consts], writes=[rB0])
                ve("tensor_scalar", negd[par], B0()[0:2, :], -1.0 / 16.0,
                   None, ALU.mult, reads=[rB0, r_d1[par]],
                   writes=[r_negd[par]])

            def s3(oc):
                par = oc % 2
                pT = vTp if par == 0 else pCbf
                rT = rD0 if par == 0 else rC0
                tposeg([(pT[:, j * 128:(j + 1) * 128],
                         ddb_all[oc][:, j * 128:(j + 1) * 128], identb[:])
                        for j in range(4)],
                       reads=[r_ddb_all[oc], r_consts], writes=[rT])
                veg([("reduce_max",
                      (mxt[par][:, 2 * j:2 * j + 2],
                       pT[:, j * 128:(j + 1) * 128].rearrange(
                           "p (h m) -> p h m", h=2)),
                      dict(axis=AX.X, negate=True)) for j in range(4)],
                    reads=[rT], writes=[r_mxt[par]])

            def s4(oc):
                par = oc % 2
                pDD, rDD = (B1(), rB1) if par == 0 else (D1(), rD1)
                pRB, rRB = (A1(), rA1) if par == 0 else (A0(), rA0)
                mmg([(C1()[0:2, j * 128:(j + 1) * 128], mxt[par][:, 2 * j:2 * j + 2],
                      identb[:], True, j == 3) for j in range(4)],
                    reads=[r_mxt[par], r_consts], writes=[rC1])
                ve("tensor_copy", negmx[par], C1()[0:2, :],
                   reads=[rC1], writes=[r_negmx[par]])
                mmg([(pDD, projbd[:], qbf_all[oc], True, False),
                     (pDD, sel01b[:], negd[par], False, False),
                     (pDD, sel01b[:], negmx[par], False, True)],
                    reads=[r_qbf_all[oc], r_projbd, r_negd[par], r_negmx[par]],
                    writes=[rDD])
                act(qf_sb[par], pDD, AF.Exp, reads=[rDD], writes=[r_qf[par]])
                ve("tensor_scalar", qf_sb[par], qf_sb[par], float(EPS_K), None,
                   ALU.add, reads=[r_qf[par]], writes=[r_qf[par]])
                mmg([(pDD, kvq[s][:, oc * 128:(oc + 1) * 128], qf_sb[par],
                      True, True),
                     (B0()[32:34, :], denpat[s][:, 2 * oc:2 * oc + 2],
                      qf_sb[par], True, True, (0, 32))],
                    reads=[r_qf[par], r_kvq[s], r_denpat[s]],
                    writes=[rDD, rB0])
                act(recip[par], B0()[32:34, :], AF.Ln,
                    reads=[rB0], writes=[r_recip[par]])
                act(d1r[par], recip[par], AF.Exp, scale=f32(-1.0),
                    reads=[r_recip[par]], writes=[r_d1[par]])
                mmg([(pRB, sel01b[:], d1r[par], True, True)],
                    reads=[r_d1[par], r_consts], writes=[rRB])
                ve("tensor_copy", t1f[par], pDD,
                   reads=[rDD], writes=[r_t1[par]])
                ve("tensor_mul", attn_sb[:, oc, :], t1f[par], pRB,
                   reads=[r_t1[par], rRB], writes=[r_attn_c[oc]])
                if debug and tile == 0 and oc == DBG_OC:
                    dma("po", dbg_qf, qf_sb[par], "sdbg", reads=[r_qf[par]])
                    dma("po", dbg_den, denf[par], "sdbg", reads=[r_denf[par]])
                    dma("po", dbg_d1, d1r[par], "sdbg", reads=[r_d1[par]])
                    dma("po", dbg_nmx, negmx[par], "sdbg", reads=[r_negmx[par]])
                    dma("po", dbg_ngd, negd[par], "sdbg", reads=[r_negd[par]])
                    dma("po", dbg_t1, t1f[par], "sdbg", reads=[r_t1[par]])

            for step in range(NCH + 2):
                if step < NCH:
                    s2(step)
                if 1 <= step <= NCH:
                    s3(step - 1)
                if step >= 2:
                    s4(step - 2)
            if debug and tile == 0:
                dma("po", dbg_xn, xn_t[:], "sdbg", reads=r_xn_t)
                dma("po", dbg_attn, attn_sb[:], "sdbg", reads=list(r_attn_c))
            S.mark(f"C{tile}_ao")
            for oc in range(NCH):
                pAO, rAO = [(A0(), rA0), (A1(), rA1),
                            (C0(), rC0)][oc % 3]
                io = take(base + 4 + oc // 2)
                wov = wslot[io][:].rearrange("p (f k j) -> p f k j", f=2, k=NCH)
                if oc == 0:
                    for kc in range(NCH):
                        mmg([(pAO, wov[:, oc % 2, kc, :], attn_sb[:, kc, :],
                              kc == 0, kc == NCH - 1)],
                            reads=[r_attn_c[kc]] + RW(io), writes=[rAO])
                else:
                    mmg([(pAO, wov[:, oc % 2, kc, :], attn_sb[:, kc, :],
                          kc == 0, kc == NCH - 1) for kc in range(NCH)],
                        reads=list(r_attn_c) + RW(io), writes=[rAO])
                ve("tensor_add", res_sb[:, oc, :], x_sb[:, oc, :], pAO,
                   reads=[r_xc[oc], rAO], writes=[r_res])
                act(x2scr[oc % 2], res_sb[:, oc, :], AF.Square,
                    reads=[r_res], writes=[r_x2[oc % 2]])
                mmg([(B0()[0:1, :], onescol_f[:], res_sb[:, oc, :],
                      oc == 0, oc == NCH - 1)],
                    reads=[r_res, r_consts], writes=[rB0])
                mmg([(B0()[32:33, :], onescol_b[:], x2scr[oc % 2],
                      oc == 0, oc == NCH - 1, (0, 32))],
                    reads=[r_x2[oc % 2], r_consts], writes=[rB0])
            ve("tensor_scalar", murow, B0()[0:1, :], 1.0 / D, None, ALU.mult,
               reads=[rB0], writes=[r_rows])
            ve("tensor_mul", musq, murow, murow, reads=[r_rows], writes=[r_rows])
            ve("scalar_tensor_tensor", varrow, B0()[32:33, :], 1.0 / D,
               musq, ALU.mult, ALU.subtract, reads=[rB0, r_rows], writes=[r_rows])
            act(rstd, varrow, AF.Ln, bias=f32(EPS_LN_IN),
                reads=[r_rows], writes=[r_rows])
            act(rstd, rstd, AF.Exp, scale=f32(-0.5), reads=[r_rows], writes=[r_rows])
            mmg([(A0(), onesrow_f[:], rstd, True, True)],
                reads=[r_rows, r_consts], writes=[rA0])
            mmg([(A1(), onesrow_f[:], murow, True, True)],
                reads=[r_rows, r_consts], writes=[rA1])
            for c in range(NCH):
                ve("tensor_sub", tscr[c % 2], res_sb[:, c, :], A1(),
                   reads=[r_res, rA1], writes=[r_tscr[c % 2]])
                ve("tensor_mul", tscr[c % 2], tscr[c % 2], A0(),
                   reads=[r_tscr[c % 2], rA0], writes=[r_tscr[c % 2]])
                ve("tensor_scalar", out1b[:, c, :], tscr[c % 2],
                   g1c[:, c:c + 1], be1c[:, c:c + 1], ALU.mult, ALU.add,
                   reads=[r_tscr[c % 2], r_consts], writes=[r_out1b])
            if debug and tile == 0:
                dma("po", dbg_out1, out1b[:], "sdbg", reads=[r_out1b])
            if c_stage < 2:
                continue
            S.mark(f"C{tile}_w1")
            li = base + NCH
            for g in range(NFC // 2):
                i = take(li)
                wv_ = wslot[i][:].rearrange("p (f k j) -> p f k j", f=2, k=NCH)
                for f_i in range(2):
                    fc = 2 * g + f_i
                    par = fc % 2
                    pH, rH = [(A0(), rA0), (A1(), rA1),
                              (C1(), rC1)][fc % 3]
                    mmg([(pH, wv_[:, f_i, kc, :], out1b[:, kc, :],
                          kc == 0, kc == NCH - 1) for kc in range(NCH)],
                        reads=RW(i) + [r_out1b], writes=[rH])
                    etok = act(escr[par][:], pH, AF.Exp, bias=b1c[:, fc:fc + 1],
                               reads=[rH, r_consts], writes=[r_escr[par]])
                    ve("tensor_scalar", rscr[par][:], pH, b1c[:, fc:fc + 1],
                       0.0, ALU.add, ALU.max,
                       reads=[rH, r_consts], writes=[r_rscr[par]], deps=[etok])
                    ve("scalar_tensor_tensor", h1_sb[:, fc, :], escr[par][:],
                       1.0, rscr[par][:], ALU.min, ALU.add,
                       reads=[r_escr[par], r_rscr[par]], writes=[r_h1])
                li += 1
            if debug and tile == 0:
                dma("po", dbg_h1, h1_sb[:, 0:4, :], "sdbg", reads=[r_h1])
            if c_stage < 3:
                continue
            S.mark(f"C{tile}_w2")
            for g in range(NFC // 2):
                i = take(li)
                wv_ = wslot[i][:].rearrange("p (a d) -> p a d", a=2)
                for k_i in range(2):
                    kc = 2 * g + k_i
                    mmg([(BANK[oc](), wv_[:, k_i, oc * 128:(oc + 1) * 128],
                          h1_sb[:, kc, :], kc == 0, kc == NFC - 1)
                         for oc in range(NCH)],
                        reads=RW(i) + [r_h1], writes=RBANK)
                li += 1
            for oc in range(NCH):
                ve("scalar_tensor_tensor", res_sb[:, oc, :], BANK[oc](),
                   b2eff[:, oc:oc + 1], out1b[:, oc, :], ALU.add, ALU.add,
                   reads=[RBANK[oc], r_b2eff, r_out1b], writes=[r_res])
            S.mark(f"C{tile}_ln2")
            emit_affine_ln(res_sb, r_res, out1b, r_out1b, g2c, be2c, EPS_LN_IN,
                           B0, rB0, A0, rA0, A1, rA1)
            if debug and tile == 0:
                dma("po", dbg_a, out1b[:], "sdbg", reads=[r_out1b])
            if c_stage < 4:
                continue
            for oc in range(NCH):
                ve("tensor_add", res_sb[:, oc, :], x_sb[:, oc, :], out1b[:, oc, :],
                   reads=[r_xc[oc], r_out1b], writes=[r_res])
            S.mark(f"C{tile}_down")
            for p in range(NHC // 2):
                ig = take(base + 40 + 2 * p)
                iu = take(base + 41 + 2 * p)
                wgv = wslot[ig][:].rearrange("p (f k j) -> p f k j", f=2, k=NCH)
                wuv = wslot[iu][:].rearrange("p (f k j) -> p f k j", f=2, k=NCH)
                for f_i in range(2):
                    hc = 2 * p + f_i
                    par = hc % 2
                    pG, rG = [(C0(), rC0), (BANK[6](), rD0),
                              (B0(), rB0)][hc % 3]
                    pU, rU = [(C1(), rC1), (D1(), rD1),
                              (B1(), rB1)][hc % 3]
                    mmg([(pG, wgv[:, f_i, kc, :], xn_t[:, kc, :],
                          kc == 0, kc == NCH - 1) for kc in range(NCH)],
                        reads=RW(ig) + r_xn_t, writes=[rG])
                    act(gscr[par][:], pG, AF.Copy,
                        reads=[rG], writes=[r_gscr[par]])
                    mmg([(pU, wuv[:, f_i, kc, :], xn_t[:, kc, :],
                          kc == 0, kc == NCH - 1) for kc in range(NCH)],
                        reads=RW(iu) + r_xn_t, writes=[rU])
                    act(uscr[par][:], pU, AF.Copy,
                        reads=[rU], writes=[r_uscr[par]])
                    act(gtan[par][:], gscr[par][:], AF.Tanh,
                        scale=f32(0.5 * INV127),
                        reads=[r_gscr[par]], writes=[r_gtan[par]])
                    ve("tensor_mul", gtan[par][:], gscr[par][:], gtan[par][:],
                       reads=[r_gscr[par], r_gtan[par]], writes=[r_gtan[par]])
                    ve("scalar_tensor_tensor", gtan[par][:], gscr[par][:], 1.0,
                       gtan[par][:], ALU.mult, ALU.add,
                       reads=[r_gscr[par], r_gtan[par]], writes=[r_gtan[par]])
                    ve("tensor_mul", hg_sb[:, hc, :], gtan[par][:], uscr[par][:],
                       reads=[r_gtan[par], r_uscr[par]], writes=[r_hg])
            if tile + 1 < c_tiles:
                emit_xtile(tile + 1)
            li = base + 64
            for g in range(NHC // 2):
                i = take(li)
                wv_ = wslot[i][:].rearrange("p (a d) -> p a d", a=2)
                for k_i in range(2):
                    kc = 2 * g + k_i
                    mmg([(BANK[oc](), wv_[:, k_i, oc * 128:(oc + 1) * 128],
                          hg_sb[:, kc, :], kc == 0, kc == NHC - 1)
                         for oc in range(NCH)],
                        reads=RW(i) + [r_hg], writes=RBANK)
                li += 1
            S.mark(f"C{tile}_fin")
            for oc in range(NCH):
                ve("scalar_tensor_tensor", res_sb[:, oc, :], BANK[oc](),
                   float(DOWN_SCALE), res_sb[:, oc, :], ALU.mult, ALU.add,
                   reads=[RBANK[oc]], writes=[r_res])
            t0 = tile * TILE
            dma("ac", outfm[:, t0:t0 + TILE].rearrange("(c p) n -> p c n", p=128),
                res_sb[:], "sout", reads=[r_res])
            if tile + 1 < c_tiles:
                xn_t = xn_sb2[(tile + 1) % 2]
                r_xn_t = r_xn2[(tile + 1) % 2]
                emit_xn(B0, rB0, BANK[6], rD0, D1, rD1)

    S.mark("END")
    import json as _json
    try:
        with open("/tmp/pe_marks.json", "w") as _f:
            _json.dump(S.marks, _f)
    except OSError:
        pass
    final_out = S.counts["sout"]
    final_dbg = S.counts["sdbg"]
    with contextlib.ExitStack() as stack:
        sems = {s_name: stack.enter_context(nc.semaphore(s_name)) for s_name in SEMS}
        block = stack.enter_context(nc.Block())

        def make_section(eng_name, tail=None):
            def section(eng):
                for (waits, fn, sem_n, inc) in S.ops[eng_name]:
                    for (ws, wv) in waits:
                        eng.wait_ge(sems[ws], wv)
                    inst = fn(eng)
                    if inst is not None and hasattr(inst, "then_inc"):
                        inst.then_inc(sems[sem_n], inc)
                if tail is not None:
                    tail(eng)
            return section

        block.sync(make_section("sp"))
        block.tensor(make_section("pe"))
        block.scalar(make_section("ac"))
        block.vector(make_section("ve"))
        def po_tail(e):
            e.wait_ge(sems["sout"], final_out)
            if final_dbg:
                e.wait_ge(sems["sdbg"], final_dbg)
        block.gpsimd(make_section("po", tail=po_tail))

    return nc


_BF = ml_dtypes.bfloat16


def _host_prep(inputs):
    shared = {
        "wqt": np.ascontiguousarray(
            np.asarray(inputs["L0_attn_Wq"]).reshape(NCH, 128, NCH, 128)
            .transpose(2, 1, 0, 3), np.float32),
        "wk": np.ascontiguousarray(inputs["L0_attn_Wk"].reshape(D, D), np.float32),
        "wv": np.ascontiguousarray(inputs["L0_attn_Wv"].reshape(D, D), np.float32),
        "wot": np.ascontiguousarray(
            np.asarray(inputs["L0_attn_Wo"]).reshape(NCH, 128, NCH, 128)
            .transpose(2, 1, 0, 3), np.float32),
        "projT": np.ascontiguousarray(np.asarray(inputs["L0_attn_proj"]).T, np.float32),
        "w1t": np.ascontiguousarray(
            np.asarray(inputs["L0_attn_W1"]).reshape(NCH, 128, NFC, 128)
            .transpose(2, 1, 0, 3), np.float32),
        "w2": np.ascontiguousarray(inputs["L0_attn_W2"], np.float32),
        "b1c": np.ascontiguousarray(
            np.asarray(inputs["L0_attn_b1"]).reshape(NFC, 128).T, np.float32),
        "b2c": np.ascontiguousarray(
            np.asarray(inputs["L0_attn_b2"]).reshape(NCH, 128).T, np.float32),
        "g1c": np.ascontiguousarray(
            np.asarray(inputs["L0_attn_ln1_g"]).reshape(NCH, 128).T, np.float32),
        "be1c": np.ascontiguousarray(
            np.asarray(inputs["L0_attn_ln1_b"]).reshape(NCH, 128).T, np.float32),
        "g2c": np.ascontiguousarray(
            np.asarray(inputs["L0_attn_ln2_g"]).reshape(NCH, 128).T, np.float32),
        "be2c": np.ascontiguousarray(
            np.asarray(inputs["L0_attn_ln2_b"]).reshape(NCH, 128).T, np.float32),
        "wgt": np.ascontiguousarray(
            np.asarray(inputs["L0_ffn_w_gate"]).reshape(NHC, 128, NCH, 128)
            .transpose(0, 3, 2, 1), np.int32),
        "wut": np.ascontiguousarray(
            np.asarray(inputs["L0_ffn_w_up"]).reshape(NHC, 128, NCH, 128)
            .transpose(0, 3, 2, 1), np.int32),
        "wdt": np.ascontiguousarray(np.asarray(inputs["L0_ffn_w_down"]).T, np.int32),
        "identb": np.eye(128, dtype=np.float32).astype(_BF),
        "identf": np.eye(128, dtype=np.float32),
        "onescol_f": np.ones((128, 1), np.float32),
        "onescol_b": np.ones((128, 1), np.float32).astype(_BF),
        "onesrow_f": np.ones((1, 128), np.float32),
        "onesblk": np.block([[np.ones((64, 1)), np.zeros((64, 1))],
                             [np.zeros((64, 1)), np.ones((64, 1))]]).astype(_BF),
        "sel01b": np.block([[np.ones((1, 64)), np.zeros((1, 64))],
                            [np.zeros((1, 64)), np.ones((1, 64))]]).astype(_BF),
        "sel01f": np.block([[np.ones((1, 64)), np.zeros((1, 64))],
                            [np.zeros((1, 64)), np.ones((1, 64))]]).astype(np.float32),
    }
    return shared


def kernel(**inputs):
    x = np.asarray(inputs["L0_attn_x"])
    shared = _host_prep(inputs)
    nc = build_program(debug=DEBUG)
    in_maps = []
    for c in range(NCORES):
        m = dict(shared)
        m["xfm"] = np.ascontiguousarray(x[c * SEQ:(c + 1) * SEQ].reshape(TOK, D).T)
        in_maps.append(m)
    res = run_bass_kernel_spmd(nc, in_maps, core_ids=list(range(NCORES)))
    outs = [res.results[c]["outfm"].T.reshape(SEQ, T, D) for c in range(NCORES)]
    return np.concatenate(outs, axis=0)


# version 31
# speedup vs baseline: 1.0116x; 1.0116x over previous
import contextlib

import numpy as np
import ml_dtypes

import concourse.bass as bass
import concourse.mybir as mybir
from concourse.bass_utils import run_bass_kernel_spmd

F32 = mybir.dt.float32
BF16 = mybir.dt.bfloat16
I32 = mybir.dt.int32
AF = mybir.ActivationFunctionType
ALU = mybir.AluOpType
AX = mybir.AxisListType

f32 = np.float32

NCORES = 8
SEQ = 4
T = 2048
TILE = 512
NTT = T // TILE
TOK = SEQ * T
D = 1024
NCH = D // 128
DFF = 4096
NFC = DFF // 128
HID = 3072
NHC = HID // 128

DN = 64.0 ** -0.25
EPS_K = 1e-6
EPS_LN_OUT = 1e-5
EPS_LN_IN = 1e-6
INV127 = 1.0 / 127.0
DOWN_SCALE = 1.0 / (2.0 * 127.0 ** 3)

DEBUG = False
DBG_OC = 0
W1_MODE = "full"
PROBE_W = False

ENGS = ("pe", "ac", "ve", "po", "sp")
DMA_SEMS = (["scst", "sres", "sout", "sccd", "sdbg"]
            + [f"sx{i}" for i in range(8)]
            + [f"sstg{i}" for i in range(4)]
            + [f"sws{i}" for i in range(6)]
            + [f"sscr_{w}" for w in ("wq", "wo", "w1", "w2", "wg", "wu", "wd")])
SEMS = ("pe", "ac", "ve", "po", "cc") + tuple(DMA_SEMS)
DEF_SEM = {"pe": "pe", "ac": "ac", "ve": "ve", "po": "po"}


class Res:
    __slots__ = ("w", "readers", "name")

    def __init__(self, name=""):
        self.w = None
        self.readers = []
        self.name = name


class Sched:
    def __init__(self):
        self.ops = {e: [] for e in ENGS}
        self.counts = {s: 0 for s in SEMS}
        self.waited = {e: {} for e in ENGS}
        self.pe_n = 0
        self.marks = []

    def mark(self, name):
        self.marks.append((name, self.pe_n))

    def op(self, eng, fn, reads=(), writes=(), deps=(), sem=None, inc=None):
        sem = sem or DEF_SEM[eng]
        if inc is None:
            inc = 16 if sem not in DEF_SEM.values() else 1
        waits = {}

        def add(tok):
            if tok is None:
                return
            s, v = tok
            if s == "pe" and eng == "pe":
                return
            if v > waits.get(s, 0):
                waits[s] = v

        for r in reads:
            add(r.w)
        for w in writes:
            add(w.w)
            for tk in w.readers:
                add(tk)
        for d in deps:
            add(d)
        ew = self.waited[eng]
        final = [(s, v) for s, v in waits.items() if v > ew.get(s, 0)]
        for s, v in final:
            ew[s] = v
        self.counts[sem] += inc
        tok = (sem, self.counts[sem])
        self.ops[eng].append((final, fn, sem, inc))
        for r in reads:
            r.readers.append(tok)
        for w in writes:
            w.w = tok
            w.readers = []
        return tok



def build_program(debug=False, phases="full", c_stage=4, c_tiles=None):
    nc = bass.Bass("TRN2", target_bir_lowering=False, debug=False, num_devices=NCORES)
    S = Sched()

    def din(name, shape, dt):
        return nc.dram_tensor(name, shape, dt, kind="ExternalInput").ap()

    xfm = din("xfm", [D, TOK], F32)
    wq_h = din("wqt", [NCH, 128, NCH, 128], F32)
    wk_h = din("wk", [D, D], F32)
    wv_h = din("wv", [D, D], F32)
    wo_h = din("wot", [NCH, 128, NCH, 128], F32)
    projT_h = din("projT", [64, 64], F32)
    w1_h = din("w1t", [NFC, 128, NCH, 128], F32)
    w2_h = din("w2", [DFF, D], F32)
    b1_h = din("b1c", [128, NFC], F32)
    b2_h = din("b2c", [128, NCH], F32)
    g1_h = din("g1c", [128, NCH], F32)
    be1_h = din("be1c", [128, NCH], F32)
    g2_h = din("g2c", [128, NCH], F32)
    be2_h = din("be2c", [128, NCH], F32)
    wg_h = din("wgt", [NHC, 128, NCH, 128], I32)
    wu_h = din("wut", [NHC, 128, NCH, 128], I32)
    wd_h = din("wdt", [HID, D], I32)
    identb_h = din("identb", [128, 128], BF16)
    identf_h = din("identf", [128, 128], F32)
    onescol_f_h = din("onescol_f", [128, 1], F32)
    onescol_b_h = din("onescol_b", [128, 1], BF16)
    onesrow_f_h = din("onesrow_f", [1, 128], F32)
    onesblk_h = din("onesblk", [128, 2], BF16)
    sel01b_h = din("sel01b", [2, 128], BF16)
    sel01f_h = din("sel01f", [2, 128], F32)

    outfm = nc.dram_tensor("outfm", [D, TOK], F32, kind="ExternalOutput").ap()

    wqscr = nc.dram_tensor("wqscr", [NCH, 128, NCH, 128], BF16)
    woscr = nc.dram_tensor("woscr", [NCH, 128, NCH, 128], BF16)
    w1scr = nc.dram_tensor("w1scr", [NFC, 128, NCH, 128], BF16)
    w2scr = nc.dram_tensor("w2scr", [DFF, D], BF16)
    wgscr = nc.dram_tensor("wgscr", [NHC, 128, NCH, 128], BF16)
    wuscr = nc.dram_tensor("wuscr", [NHC, 128, NCH, 128], BF16)
    wdscr = nc.dram_tensor("wdscr", [HID, D], BF16)
    ccbuf = nc.dram_tensor("ccbuf", [1, 16384], F32)

    sb = nc.alloc_sbuf_tensor
    wkr = sb("wkr", [128, NCH, 1024], BF16)
    wvr = sb("wvr", [128, NCH, 1024], BF16)
    projbd = sb("projbd", [128, 128], BF16)
    identb = sb("identb_s", [128, 128], BF16)
    identf = sb("identf_s", [128, 128], F32)
    onescol_f = sb("onescol_f_s", [128, 1], F32)
    onescol_b = sb("onescol_b_s", [128, 1], BF16)
    onesrow_f = sb("onesrow_f_s", [1, 128], F32)
    onesblk = sb("onesblk_s", [128, 2], BF16)
    sel01b = sb("sel01b_s", [2, 128], BF16)
    sel01f = sb("sel01f_s", [2, 128], F32)
    b1c = sb("b1c_s", [128, NFC], F32)
    b2c = sb("b2c_s", [128, NCH], F32)
    b2eff = sb("b2eff", [128, NCH], F32)
    g1c = sb("g1c_s", [128, NCH], F32)
    be1c = sb("be1c_s", [128, NCH], F32)
    g2c = sb("g2c_s", [128, NCH], F32)
    be2c = sb("be2c_s", [128, NCH], F32)
    stgproj = sb("stgproj", [64, 64], F32)

    x_sb = sb("x_sb", [128, NCH, TILE], F32)
    xn_sb2 = [sb(f"xn_sb{i}", [128, NCH, TILE], BF16) for i in range(2)]
    res_sb = sb("res_sb", [128, NCH, TILE], F32)
    out1b = sb("out1b", [128, NCH, TILE], BF16)
    h1_sb = sb("h1_sb", [128, NFC, TILE], BF16)
    hg_sb = sb("hg_sb", [128, NHC, TILE], BF16)
    attn_sb = sb("attn_sb", [128, NCH, TILE], BF16)
    gscr = [sb(f"gscr{i}", [128, TILE], BF16) for i in range(2)]
    uscr = [sb(f"uscr{i}", [128, TILE], BF16) for i in range(2)]
    gtan = [sb(f"gtan{i}", [128, TILE], BF16) for i in range(2)]
    escr = [sb(f"escr{i}", [128, TILE], BF16) for i in range(2)]
    rscr = [sb(f"rscr{i}", [128, TILE], BF16) for i in range(2)]

    kvq = [sb(f"kvq{s}", [128, 1024], BF16) for s in range(SEQ)]
    denpat = [sb(f"denpat{s}", [128, 16], BF16) for s in range(SEQ)]
    ksum = [sb(f"ksum{s}", [128, NCH], F32) for s in range(SEQ)]
    vsum = [sb(f"vsum{s}", [128, NCH], F32) for s in range(SEQ)]
    mxrun = sb("mxrun", [128, 1], F32)
    mxg = sb("mxg", [1, 1], F32)
    emx1 = sb("emx1", [1, 1], F32)
    emx_col = sb("emx_col", [128, 1], F32)
    zero_sb = sb("zero_sb", [128, 128], F32)

    kbf = [h1_sb[:, 0, :], h1_sb[:, 1, :]]
    vbf = [h1_sb[:, 2, :], h1_sb[:, 3, :]]
    kfb = [uscr[0][:], uscr[1][:]]
    kfT = [gtan[0][:], gtan[1][:]]
    vT = [escr[0][:], escr[1][:]]
    k2b = [out1b[:, 2, :], out1b[:, 3, :]]
    negd = [h1_sb[0:2, 16, :], h1_sb[0:2, 17, :]]
    negmx = [h1_sb[0:2, 18, :], h1_sb[0:2, 19, :]]
    d1r = [h1_sb[0:2, 20, :], h1_sb[0:2, 21, :]]
    denf = d1r
    sesb = [h1_sb[0:2, 22, :], h1_sb[0:2, 23, :]]
    mxt = [sb(f"mxt{i}", [128, 8], BF16) for i in range(2)]
    qbf_all = [h1_sb[:, i, :] for i in range(8)]
    ddb_all = [h1_sb[:, 8 + i, :] for i in range(8)]
    kbf_all = qbf_all
    vbf_all = ddb_all
    qf_sb = [out1b[:, 0, :], out1b[:, 1, :]]
    k2a = [out1b[:, i, :] for i in range(8)]

    h1f32 = h1_sb.bitcast(F32)
    o1f32 = out1b.bitcast(F32)
    t1f = [o1f32[:, 4:6, :].rearrange("p a b -> p (a b)"),
           o1f32[:, 6:8, :].rearrange("p a b -> p (a b)")]
    recip = [h1f32[0:2, 24:26, :].rearrange("p a b -> p (a b)"),
             h1f32[0:2, 26:28, :].rearrange("p a b -> p (a b)")]
    tscr = [h1f32[:, 28:30, :].rearrange("p a b -> p (a b)"),
            h1f32[:, 30:32, :].rearrange("p a b -> p (a b)")]
    x2scr = [h1_sb[:, 12, :], h1_sb[:, 13, :]]
    ksum_t = [sb(f"ksum_t{i}", [128, 1], F32) for i in range(2)]
    vsum_t = [sb(f"vsum_t{i}", [128, 1], F32) for i in range(2)]
    mxtt = [sb(f"mxtt{i}", [128, 1], F32) for i in range(2)]
    murow = sb("murow_t", [1, TILE], F32)[:]
    musq = sb("musq_t", [1, TILE], F32)[:]
    varrow = sb("varrow_t", [1, TILE], F32)[:]
    rstd = sb("rstd_t", [1, TILE], F32)[:]

    NSLOT = 6
    wslot = [sb(f"wslot{i}", [128, 2048], BF16) for i in range(NSLOT)]
    stg = [wslot[i][:, 0:1024] for i in range(4)]

    pA = nc.alloc_psum_tensor("pA", [128, 1024], F32)
    pB = nc.alloc_psum_tensor("pB", [128, 1024], F32)
    pC = nc.alloc_psum_tensor("pC", [128, 1024], F32)
    pD = nc.alloc_psum_tensor("pD", [128, 512], F32)
    vTp = nc.alloc_psum_tensor("vTp", [128, 1024], BF16)
    vTf = vTp.bitcast(F32)
    pDbf = pD.bitcast(BF16)
    pCbf = pC.bitcast(BF16)

    A0 = lambda: pA[:, 0:512]
    A1 = lambda: pA[:, 512:1024]
    B0 = lambda: pB[:, 0:512]
    B1 = lambda: pB[:, 512:1024]
    C0 = lambda: pC[:, 0:512]
    C1 = lambda: pC[:, 512:1024]
    D1 = lambda: pD[:, 0:512]
    BANK = [A0, A1, B0, B1, C0, C1, lambda: vTf[:, 0:512], D1]

    rA0, rA1, rB0, rB1, rC0, rC1, rD0, rD1 = (Res(n) for n in
        ("A0", "A1", "B0", "B1", "C0", "C1", "D0", "D1"))
    RBANK = [rA0, rA1, rB0, rB1, rC0, rC1, rD0, rD1]

    r_x, r_res, r_out1b, r_attn, r_h1, r_hg = (Res(n) for n in
        ("x", "res", "out1b", "attn", "h1", "hg"))
    r_xc = [Res(f"xc{c}") for c in range(NCH)]
    r_xn2 = [[Res(f"xn0_{c}") for c in range(NCH)],
             [Res(f"xn1_{c}") for c in range(NCH)]]
    r_kvq = [Res(f"kvq{s}") for s in range(SEQ)]
    r_denpat = [Res(f"denpat{s}") for s in range(SEQ)]
    r_ksum = [Res(f"ksum{s}") for s in range(SEQ)]
    r_vsum = [Res(f"vsum{s}") for s in range(SEQ)]
    r_mxrun = Res("mxrun")
    r_mx = Res("mx")
    r_emx = Res("emx")
    r_wslot = [Res(f"wslot{i}") for i in range(NSLOT)]
    RW = (lambda i: []) if PROBE_W else (lambda i: [r_wslot[i]])
    r_stg = r_wslot
    two = lambda n: [Res(n + "0"), Res(n + "1")]
    r_kbf, r_vbf, r_k2, r_kfb, r_kfT, r_vT = (two(n) for n in
        ("kbf", "vbf", "k2", "kfb", "kfT", "vT"))
    r_mxt, r_qf = two("mxt"), two("qf")
    r_k2a = [Res(f"k2a{i}") for i in range(8)]
    r_attn_c = [Res(f"attnc{i}") for i in range(8)]
    r_qbf_all = [Res(f"qbf{i}") for i in range(8)]
    r_ddb_all = [Res(f"ddbA{i}") for i in range(8)]
    r_kbf_all = r_qbf_all
    r_vbf_all = r_ddb_all
    r_negd = two("negd")
    r_negmx = two("negmx")
    r_sesb, r_d1, r_recip = (two(n) for n in ("sesb", "d1", "recip"))
    r_denf = r_d1
    r_t1 = two("t1")
    r_tscr, r_x2, r_escr, r_rscr, r_ksumt, r_vsumt, r_mxtt = (two(n) for n in
        ("tscr", "x2", "escr", "rscr", "ksumt", "vsumt", "mxtt"))
    r_gscr, r_uscr, r_gtan = two("gscr"), two("uscr"), two("gtan")
    r_kfb, r_kfT, r_vT = r_uscr, r_gtan, r_escr
    r_rows = Res("rows")
    r_consts = Res("consts")
    r_b2eff = Res("b2eff")
    r_projbd = Res("projbd")
    r_wres = Res("wres")
    r_scr = {"w1": Res("w1scr"), "w2": Res("w2scr"), "wg": Res("wgscr"),
             "wu": Res("wuscr"), "wd": Res("wdscr"), "wq": Res("wqscr"),
             "wo": Res("woscr")}
    r_cc = Res("ccbuf")

    if debug:
        dbg_xn = nc.dram_tensor("dbg_xn", [128, NCH, TILE], BF16, kind="ExternalOutput").ap()
        dbg_kv = nc.dram_tensor("dbg_kv", [128, 1024], BF16, kind="ExternalOutput").ap()
        dbg_ks = nc.dram_tensor("dbg_ks", [128, NCH], F32, kind="ExternalOutput").ap()
        dbg_vs = nc.dram_tensor("dbg_vs", [128, NCH], F32, kind="ExternalOutput").ap()
        dbg_mx = nc.dram_tensor("dbg_mx", [1, 1], F32, kind="ExternalOutput").ap()
        dbg_attn = nc.dram_tensor("dbg_attn", [128, NCH, TILE], BF16, kind="ExternalOutput").ap()
        dbg_out1 = nc.dram_tensor("dbg_out1", [128, NCH, TILE], BF16, kind="ExternalOutput").ap()
        dbg_h1 = nc.dram_tensor("dbg_h1", [128, 4, TILE], BF16, kind="ExternalOutput").ap()
        dbg_a = nc.dram_tensor("dbg_a", [128, NCH, TILE], BF16, kind="ExternalOutput").ap()
        dbg_qf = nc.dram_tensor("dbg_qf", [128, TILE], BF16, kind="ExternalOutput").ap()
        dbg_den = nc.dram_tensor("dbg_den", [2, TILE], F32, kind="ExternalOutput").ap()
        dbg_d1 = nc.dram_tensor("dbg_d1", [2, TILE], F32, kind="ExternalOutput").ap()
        dbg_nmx = nc.dram_tensor("dbg_nmx", [2, TILE], F32, kind="ExternalOutput").ap()
        dbg_ngd = nc.dram_tensor("dbg_ngd", [2, TILE], F32, kind="ExternalOutput").ap()
        dbg_t1 = nc.dram_tensor("dbg_t1", [128, TILE], F32, kind="ExternalOutput").ap()
        dbg_qb = nc.dram_tensor("dbg_qb", [128, TILE], BF16, kind="ExternalOutput").ap()
        dbg_wq = nc.dram_tensor("dbg_wq", [128, 2048], BF16, kind="ExternalOutput").ap()

    def dma(q, out_ap, in_ap, sem, reads=(), writes=(), deps=(), slow=False):
        kw = {"allow_slow_non_contiguous": True} if slow else {}
        return S.op(q, lambda e, o=out_ap, i=in_ap, k=kw: e.dma_start(out=o, in_=i, **k),
                    reads=reads, writes=writes, deps=deps, sem=sem, inc=16)

    def mmg(items, reads=(), writes=(), deps=()):
        S.pe_n += len(items)
        def fn(e, items=items):
            last = None
            for it in items:
                o, l, r, st, sp = it[:5]
                kw = dict(start=st, stop=sp, skip_group_check=True)
                if len(it) > 5:
                    kw["tile_position"] = it[5]
                last = e.matmul(o, l, r, **kw)
            return last
        return S.op("pe", fn, reads=reads, writes=writes, deps=deps)

    def tpose(out_ap, in_ap, ident, reads=(), writes=(), deps=()):
        S.pe_n += 1
        return S.op("pe", lambda e, o=out_ap, i=in_ap, d=ident: e.transpose(o, i, d),
                    reads=reads, writes=writes, deps=deps)

    def tposeg(items, reads=(), writes=(), deps=()):
        S.pe_n += len(items)
        def fn(e, items=items):
            last = None
            for o, i, d in items:
                last = e.transpose(o, i, d)
            return last
        return S.op("pe", fn, reads=reads, writes=writes, deps=deps)

    def veg(calls, reads=(), writes=(), deps=()):
        def fn(e, calls=calls):
            last = None
            for m, a, k in calls:
                last = getattr(e, m)(*a, **k)
            return last
        return S.op("ve", fn, reads=reads, writes=writes, deps=deps)

    def act(out_ap, in_ap, func, reads=(), writes=(), deps=(), **kw):
        return S.op("ac",
                    lambda e, o=out_ap, i=in_ap, fu=func, k=dict(kw):
                        e.activation(out=o, in_=i, func=fu, **k),
                    reads=reads, writes=writes, deps=deps)

    def ve(meth, *args, reads=(), writes=(), deps=(), **kw):
        return S.op("ve", lambda e, m=meth, a=args, k=dict(kw): getattr(e, m)(*a, **k),
                    reads=reads, writes=writes, deps=deps)

    def po(meth, *args, reads=(), writes=(), deps=(), **kw):
        return S.op("po", lambda e, m=meth, a=args, k=dict(kw): getattr(e, m)(*a, **k),
                    reads=reads, writes=writes, deps=deps, sem="po", inc=1)

    for dst, src in [
        (identb[:], identb_h), (identf[:], identf_h),
        (onescol_f[:], onescol_f_h), (onescol_b[:], onescol_b_h),
        (onesrow_f[:], onesrow_f_h), (onesblk[:], onesblk_h),
        (sel01b[:], sel01b_h), (sel01f[:], sel01f_h),
        (b1c[:], b1_h), (b2c[:], b2_h),
        (g1c[:], g1_h), (be1c[:], be1_h), (g2c[:], g2_h), (be2c[:], be2_h),
        (stgproj[:], projT_h),
    ]:
        dma("sp", dst, src, "scst", writes=[r_consts])

    for kc in range(NCH):
        dma("po", wkr[:, kc, :], wk_h[kc * 128:(kc + 1) * 128, :], "sres", writes=[r_wres])
    for kc in range(NCH):
        dma("po", wvr[:, kc, :], wv_h[kc * 128:(kc + 1) * 128, :], "sres", writes=[r_wres])
    wkwv_done = r_wres.w

    po("memset", projbd[:], 0.0, writes=[r_projbd])
    ve("tensor_scalar", projbd[0:64, 0:64], stgproj[:], float(DN), None, ALU.mult,
       reads=[r_consts], writes=[r_projbd])
    ve("tensor_scalar", projbd[64:128, 64:128], stgproj[:], float(DN), None, ALU.mult,
       reads=[r_consts], writes=[r_projbd])

    po("memset", zero_sb[:], 0.0, writes=[r_cc])
    dma("po", ccbuf.ap().rearrange("o (p f) -> p (o f)", p=128), zero_sb[:],
        "sccd", reads=[r_cc], writes=[r_cc])
    for s in range(SEQ):
        po("memset", kvq[s][:], 0.0, writes=[r_kvq[s]])
        po("memset", denpat[s][:], 0.0, writes=[r_denpat[s]])
        po("memset", ksum[s][:], 0.0, writes=[r_ksum[s]])
        po("memset", vsum[s][:], 0.0, writes=[r_vsum[s]])
    ve("memset", mxrun[:], -1e30, writes=[r_mxrun])

    def emit_xtile(tile):
        t0 = tile * TILE
        xv = xfm[:, t0:t0 + TILE].rearrange("(c p) n -> p c n", p=128)
        for c in range(NCH):
            dma("ac", x_sb[:, c, :], xv[:, c, :],
                f"sx{c}", writes=[r_xc[c]])

    def emit_stats(src_sb, r_src, eps, pS, rS, fill=None, x2b=None):
        x2t, r_x2t = x2b if x2b is not None else (x2scr, r_x2)
        for c in range(NCH):
            rs = r_src[c] if isinstance(r_src, list) else r_src
            act(x2t[c % 2], src_sb[:, c, :], AF.Square,
                reads=[rs], writes=[r_x2t[c % 2]])
            mmg([(pS()[0:1, :], onescol_f[:], src_sb[:, c, :],
                  c == 0, c == NCH - 1),
                 (pS()[32:33, :], onescol_b[:], x2t[c % 2],
                  c == 0, c == NCH - 1, (0, 32))],
                reads=[rs, r_x2t[c % 2], r_consts], writes=[rS])
        ve("tensor_scalar", murow, pS()[0:1, :], 1.0 / D, None, ALU.mult,
           reads=[rS], writes=[r_rows])
        ve("tensor_mul", musq, murow, murow,
           reads=[r_rows], writes=[r_rows])
        ve("scalar_tensor_tensor", varrow, pS()[32:33, :], 1.0 / D,
           musq, ALU.mult, ALU.subtract,
           reads=[rS, r_rows], writes=[r_rows])
        if fill is not None:
            fill()
        act(rstd, varrow, AF.Ln, bias=f32(eps),
            reads=[r_rows], writes=[r_rows])
        act(rstd, rstd, AF.Exp, scale=f32(-0.5),
            reads=[r_rows], writes=[r_rows])

    def emit_xn(pS, rS, pX, rX, pY, rY, x2b=None):
        emit_stats(x_sb, r_xc, EPS_LN_OUT, pS, rS, x2b=x2b)
        mmg([(pX(), onesrow_f[:], rstd, True, True),
             (pY(), onesrow_f[:], murow, True, True)],
            reads=[r_rows, r_consts], writes=[rX, rY])
        for c in range(NCH):
            ve("tensor_sub", tscr[c % 2], x_sb[:, c, :], pY(),
               reads=[r_xc[c], rY], writes=[r_tscr[c % 2]])
            ve("tensor_mul", xn_t[:, c, :], tscr[c % 2], pX(),
               reads=[r_tscr[c % 2], rX], writes=[r_xn_t[c]])

    def emit_affine_ln(src_sb, r_src, dst, r_dst, g_col, b_col, eps,
                       pS, rS, pX, rX, pY, rY, fill=None):
        emit_stats(src_sb, r_src, eps, pS, rS, fill=fill)
        mmg([(pX(), onesrow_f[:], rstd, True, True),
             (pY(), onesrow_f[:], murow, True, True)],
            reads=[r_rows, r_consts], writes=[rX, rY])
        for c in range(NCH):
            ve("tensor_sub", tscr[c % 2], src_sb[:, c, :], pY(),
               reads=[r_src, rY], writes=[r_tscr[c % 2]])
            ve("tensor_mul", tscr[c % 2], tscr[c % 2], pX(),
               reads=[r_tscr[c % 2], rX], writes=[r_tscr[c % 2]])
            ve("tensor_scalar", dst[:, c, :], tscr[c % 2],
               g_col[:, c:c + 1], b_col[:, c:c + 1], ALU.mult, ALU.add,
               reads=[r_tscr[c % 2], r_consts], writes=[r_dst])

    for s in range(SEQ):
        for t in range(NTT):
            tile = s * NTT + t
            xn_t = xn_sb2[tile % 2]
            r_xn_t = r_xn2[tile % 2]
            S.mark(f"AB{tile}")
            if tile == 0:
                emit_xtile(0)
            emit_xn(C0, rC0, C1, rC1, D1, rD1,
                    x2b=([gscr[0][:], gscr[1][:]], r_gscr))
            for oc in range(NCH):
                par = oc % 2
                pK, rK = (B0(), rB0) if par == 0 else (B1(), rB1)
                pV, rV = (C0(), rC0) if par == 0 else (C1(), rC1)
                if oc == 0:
                    for kc in range(NCH):
                        mmg([(pK, wkr[:, kc, oc * 128:(oc + 1) * 128],
                              xn_t[:, kc, :], kc == 0, kc == NCH - 1)],
                            reads=[r_xn_t[kc], r_wres], writes=[rK],
                            deps=[wkwv_done])
                        mmg([(pV, wvr[:, kc, oc * 128:(oc + 1) * 128],
                              xn_t[:, kc, :], kc == 0, kc == NCH - 1)],
                            reads=[r_xn_t[kc], r_wres], writes=[rV],
                            deps=[wkwv_done])
                else:
                    mmg([(pK, wkr[:, kc, oc * 128:(oc + 1) * 128],
                          xn_t[:, kc, :], kc == 0, kc == NCH - 1)
                         for kc in range(NCH)],
                        reads=list(r_xn_t) + [r_wres], writes=[rK],
                        deps=[wkwv_done])
                    mmg([(pV, wvr[:, kc, oc * 128:(oc + 1) * 128],
                          xn_t[:, kc, :], kc == 0, kc == NCH - 1)
                         for kc in range(NCH)],
                        reads=list(r_xn_t) + [r_wres], writes=[rV],
                        deps=[wkwv_done])
                act(kbf_all[oc], pK, AF.Copy,
                    reads=[rK], writes=[r_kbf_all[oc]])
                act(vbf_all[oc], pV, AF.Copy, accum_out=vsum_t[par][:],
                    reads=[rV], writes=[r_vbf_all[oc], r_vsumt[par]])
                ve("scalar_tensor_tensor", vsum[s][:, oc:oc + 1], vsum_t[par][:],
                   float(EPS_K), vsum[s][:, oc:oc + 1], ALU.mult, ALU.add,
                   reads=[r_vsumt[par]], writes=[r_vsum[s]])
            emit_xtile((tile + 1) % (SEQ * NTT))
            for oc in range(NCH):
                ve("tensor_mul", k2a[oc], kbf_all[oc], kbf_all[oc],
                   reads=[r_kbf_all[oc]], writes=[r_k2a[oc]])

            def ab2a(oc):
                par = oc % 2
                pDD, rDD = (B0(), rB0) if par == 0 else (B1(), rB1)
                mmg([(pDD, projbd[:], kbf_all[oc], True, True),
                     (C0()[0:2, :], onesblk[:], k2a[oc], True, True)],
                    reads=[r_kbf_all[oc], r_projbd, r_k2a[oc], r_consts],
                    writes=[rDD, rC0])
                ve("reduce_max", mxtt[par][:], pDD, reads=[rDD],
                   writes=[r_mxtt[par]], axis=AX.X)
                ve("tensor_max", mxrun[:], mxrun[:], mxtt[par][:],
                   reads=[r_mxtt[par]], writes=[r_mxrun])
                ve("tensor_scalar", negd[par], C0()[0:2, :], -1.0 / 16.0,
                   None, ALU.mult, reads=[rC0], writes=[r_negd[par]])

            def ab2b(oc):
                par = oc % 2
                pDD, rDD = (B0(), rB0) if par == 0 else (B1(), rB1)
                mmg([(pDD, sel01b[:], negd[par], False, True)],
                    reads=[r_negd[par], r_consts], writes=[rDD])
                act(kfb[par], pDD, AF.Exp, accum_out=ksum_t[par][:],
                    reads=[rDD], writes=[r_kfb[par], r_ksumt[par]])
                ve("tensor_add", ksum[s][:, oc:oc + 1], ksum[s][:, oc:oc + 1],
                   ksum_t[par][:], reads=[r_ksumt[par]], writes=[r_ksum[s]])

            def ab3(oc):
                par = oc % 2
                tposeg([(vTp[:, j * 128:(j + 1) * 128],
                         kfb[par][:, j * 128:(j + 1) * 128], identb[:])
                        for j in range(4)],
                       reads=[r_kfb[par], r_consts], writes=[rD0])
                act(kfT[par], vTp[:, 0:512], AF.Copy,
                    reads=[rD0], writes=[r_kfT[par]])
                tposeg([(pDbf[:, j * 128:(j + 1) * 128],
                         vbf_all[oc][:, j * 128:(j + 1) * 128], identb[:])
                        for j in range(4)],
                       reads=[r_vbf_all[oc], r_consts], writes=[rD1])
                ve("tensor_copy", vT[par], pDbf[:, 0:512],
                   reads=[rD1], writes=[r_vT[par]])

            def ab4(oc):
                par = oc % 2
                kv_first = (t == 0 and oc in (0, 4))
                rKV = rA0 if oc < 4 else rA1
                mmg([(pA[:, oc * 128:(oc + 1) * 128],
                      kfT[par][:, j * 128:(j + 1) * 128],
                      vT[par][:, j * 128:(j + 1) * 128],
                      kv_first and j == 0, True) for j in range(4)],
                    reads=[r_kfT[par], r_vT[par]], writes=[rKV])

            for step in range(NCH + 2):
                if step < NCH:
                    ab2a(step)
                if 1 <= step <= NCH:
                    ab3(step - 1)
                if step < NCH:
                    ab2b(step)
                if step >= 2:
                    ab4(step - 2)
        for oc in range(NCH):
            rKV = rA0 if oc < 4 else rA1
            act(kvq[s][0:64, oc * 128:oc * 128 + 64],
                pA[0:64, oc * 128:oc * 128 + 64], AF.Copy,
                reads=[rKV], writes=[r_kvq[s]])
            act(kvq[s][64:128, oc * 128 + 64:oc * 128 + 128],
                pA[64:128, oc * 128 + 64:oc * 128 + 128], AF.Copy,
                reads=[rKV], writes=[r_kvq[s]])
            ve("tensor_copy", denpat[s][0:64, 2 * oc:2 * oc + 1],
               ksum[s][0:64, oc:oc + 1], reads=[r_ksum[s]], writes=[r_denpat[s]])
            ve("tensor_copy", denpat[s][64:128, 2 * oc + 1:2 * oc + 2],
               ksum[s][64:128, oc:oc + 1], reads=[r_ksum[s]], writes=[r_denpat[s]])

    if debug:
        dma("po", dbg_kv, kvq[0][:], "sdbg", reads=[r_kvq[0]])
        dma("po", dbg_ks, ksum[0][:], "sdbg", reads=[r_ksum[0]])
        dma("po", dbg_vs, vsum[0][:], "sdbg", reads=[r_vsum[0]])

    def conv(src_ap, dst_ap, sl, which, extra=None):
        dma("po", stg[sl][:], src_ap, f"sstg{sl}", writes=[r_stg[sl]])
        if extra is not None:
            extra(sl)
        dma("sp", dst_ap, stg[sl][:], f"sscr_{which}",
            reads=[r_stg[sl]], writes=[r_scr[which]])

    for i in range(NCH):
        conv(wq_h[i].rearrange("p k j -> p (k j)"),
             wqscr.ap()[i].rearrange("p k j -> p (k j)"), i % 4, "wq")
    for i in range(NCH):
        conv(wo_h[i].rearrange("p k j -> p (k j)"),
             woscr.ap()[i].rearrange("p k j -> p (k j)"), i % 4, "wo")
    for i in range(NFC):
        conv(w1_h[i].rearrange("p k j -> p (k j)"),
             w1scr.ap()[i].rearrange("p k j -> p (k j)"), i % 4, "w1")
    mmg([(pB[:, 0:128], onesrow_f[:], identf[0:1, :], True, True)],
        reads=[r_consts], writes=[rB0])
    for i in range(NFC):
        def w2_extra(sl, i=i):
            mmg([(pB[:, c:c + 1], stg[sl][:, c * 128:(c + 1) * 128], onescol_b[:],
                  False, i == NFC - 1) for c in range(NCH)],
                reads=[r_stg[sl], r_consts], writes=[rB0])
        conv(w2_h[i * 128:(i + 1) * 128, :],
             w2scr.ap()[i * 128:(i + 1) * 128, :], i % 4, "w2", extra=w2_extra)
    for c in range(NCH):
        ve("tensor_sub", b2eff[:, c:c + 1], b2c[:, c:c + 1], pB[:, c:c + 1],
           reads=[rB0, r_consts], writes=[r_b2eff])
    for i in range(NHC):
        conv(wg_h[i].rearrange("p k j -> p (k j)"),
             wgscr.ap()[i].rearrange("p k j -> p (k j)"), i % 4, "wg")
    for i in range(NHC):
        conv(wu_h[i].rearrange("p k j -> p (k j)"),
             wuscr.ap()[i].rearrange("p k j -> p (k j)"), i % 4, "wu")
    for i in range(HID // 128):
        conv(wd_h[i * 128:(i + 1) * 128, :],
             wdscr.ap()[i * 128:(i + 1) * 128, :], i % 4, "wd")


    S.mark("ABend")
    tpose(B0()[0:1, 0:128], mxrun[:], identf[:],
          reads=[r_mxrun, r_consts], writes=[rB0])
    ve("reduce_max", mxg[0:1, 0:1], B0()[0:1, 0:128], reads=[rB0],
       writes=[r_mx], axis=AX.X)
    dma("po", ccbuf.ap()[0:1, 0:1], mxg[0:1, 0:1], "sccd", reads=[r_mx], writes=[r_cc])
    S.op("po", lambda e: e.collective_compute(
        "AllReduce", ALU.max, replica_groups=[list(range(NCORES))],
        ins=[ccbuf.ap()], outs=[ccbuf.ap()]),
        reads=[r_cc], writes=[r_cc], sem="cc", inc=1)
    dma("po", mxg[0:1, 0:1], ccbuf.ap()[0:1, 0:1], "sccd", reads=[r_cc], writes=[r_mx])
    if debug:
        dma("po", dbg_mx, mxg[0:1, 0:1], reads=[r_mx])
    act(emx1[0:1, 0:1], mxg[0:1, 0:1], AF.Exp, scale=f32(-1.0),
        reads=[r_mx], writes=[r_emx])
    mmg([(B0()[:, 0:1], onesrow_f[:], emx1[0:1, 0:1], True, True)],
        reads=[r_emx, r_consts], writes=[rB0])
    act(emx_col[:], B0()[:, 0:1], AF.Copy, reads=[rB0], writes=[r_emx])
    for s in range(SEQ):
        ve("tensor_scalar", kvq[s][:], kvq[s][:], emx_col[:, 0:1], None,
           ALU.mult, reads=[r_emx, r_kvq[s]], writes=[r_kvq[s]])
        ve("tensor_scalar", denpat[s][:], denpat[s][:], emx_col[:, 0:1],
           float(EPS_K * T), ALU.mult, ALU.add,
           reads=[r_emx, r_denpat[s]], writes=[r_denpat[s]])
        cvA = ccbuf.ap()[0:1, 2048 + s * 1024:2048 + s * 1024 + 512]
        cvB = ccbuf.ap()[0:1, 2048 + s * 1024 + 512:2048 + (s + 1) * 1024]
        dma("po", cvA.rearrange("x (p c) -> (x p) c", p=128), vsum[s][:, 0:4],
            "sccd", reads=[r_vsum[s]], writes=[r_cc])
        dma("po", cvB.rearrange("x (p c) -> (x p) c", p=128), vsum[s][:, 4:8],
            "sccd", reads=[r_vsum[s]], writes=[r_cc])
        dma("po", murow, cvA, "sccd", reads=[r_cc], writes=[r_rows])
        dma("po", musq, cvB, "sccd", reads=[r_cc], writes=[r_rows])
        mmg([(B0(), onesrow_f[:], murow.rearrange("x (p c) -> x c p", p=128),
              True, True),
             (B1(), onesrow_f[:], musq.rearrange("x (p c) -> x c p", p=128),
              True, True)],
            reads=[r_rows, r_consts], writes=[rB0, rB1])
        act(gscr[0][:], B0(), AF.Copy, reads=[rB0], writes=[r_gscr[0]])
        act(gscr[1][:], B1(), AF.Copy, reads=[rB1], writes=[r_gscr[1]])
        ve("tensor_add", kvq[s][:, 0:512], kvq[s][:, 0:512], gscr[0][:],
           reads=[r_gscr[0]], writes=[r_kvq[s]])
        ve("tensor_add", kvq[s][:, 512:1024], kvq[s][:, 512:1024], gscr[1][:],
           reads=[r_gscr[1]], writes=[r_kvq[s]])

    LOOKAHEAD = 4
    load_fns = []
    issued = []
    slot_ctr = [0]

    def _issue_next():
        i = slot_ctr[0] % NSLOT
        slot_ctr[0] += 1
        load_fns[len(issued)](i)
        issued.append(i)

    def take(idx):
        while len(issued) <= min(idx + LOOKAHEAD, len(load_fns) - 1):
            _issue_next()
        return issued[idx]

    def mk_blk2(scr_t, g, which):
        def f(i):
            dma("sp" if i % 2 == 0 else "po",
                wslot[i][:].rearrange("p (f kj) -> p f kj", f=2),
                scr_t.ap()[2 * g:2 * g + 2].rearrange("f p k j -> p f (k j)"),
                f"sws{i}", reads=[r_scr[which]], writes=[r_wslot[i]])
        return f

    def mk_rows2(scr_t, g, which):
        def f(i):
            dma("sp" if i % 2 == 0 else "po",
                wslot[i][:].rearrange("p (a d) -> p a d", a=2),
                scr_t.ap()[g * 256:(g + 1) * 256, :].rearrange("(a p) d -> p a d", p=128),
                f"sws{i}", reads=[r_scr[which]], writes=[r_wslot[i]])
        return f

    def tile_load_plan():
        plan = []
        for g in range(NCH // 2):
            plan.append(mk_blk2(wqscr, g, "wq"))
        for g in range(NCH // 2):
            plan.append(mk_blk2(woscr, g, "wo"))
        for g in range(NFC // 2):
            plan.append(mk_blk2(w1scr, g, "w1"))
        for g in range(NFC // 2):
            plan.append(mk_rows2(w2scr, g, "w2"))
        for g in range(NHC // 2):
            plan.append(mk_blk2(wgscr, g, "wg"))
            plan.append(mk_blk2(wuscr, g, "wu"))
        for g in range(HID // 256):
            plan.append(mk_rows2(wdscr, g, "wd"))
        return plan

    n_per_tile = None
    for _tile in range(SEQ * NTT):
        p = tile_load_plan()
        if n_per_tile is None:
            n_per_tile = len(p)
        load_fns.extend(p)

    if c_tiles is None:
        c_tiles = SEQ * NTT if phases == "full" else 0
    for s in range(SEQ):
        for t in range(NTT):
            tile = s * NTT + t
            if tile >= c_tiles:
                continue
            base = tile * n_per_tile
            xn_t = xn_sb2[tile % 2]
            r_xn_t = r_xn2[tile % 2]
            S.mark(f"C{tile}_xn")
            if tile == 0:
                emit_xn(B0, rB0, A0, rA0, A1, rA1)

            for oc in range(NCH):
                pQ, rQ = (A0(), rA0) if oc % 2 == 0 else (A1(), rA1)
                iq = take(base + oc // 2)
                wqv = wslot[iq][:].rearrange("p (f k j) -> p f k j", f=2, k=NCH)
                if oc == 0:
                    for kc in range(NCH):
                        mmg([(pQ, wqv[:, oc % 2, kc, :], xn_t[:, kc, :],
                              kc == 0, kc == NCH - 1)],
                            reads=[r_xn_t[kc]] + RW(iq), writes=[rQ])
                else:
                    mmg([(pQ, wqv[:, oc % 2, kc, :], xn_t[:, kc, :],
                          kc == 0, kc == NCH - 1) for kc in range(NCH)],
                        reads=list(r_xn_t) + RW(iq), writes=[rQ])
                act(qbf_all[oc], pQ, AF.Copy, reads=[rQ], writes=[r_qbf_all[oc]])

            def s2(oc):
                par = oc % 2
                pDD, rDD = (B1(), rB1) if par == 0 else (D1(), rD1)
                ve("tensor_mul", k2b[par], qbf_all[oc], qbf_all[oc],
                   reads=[r_qbf_all[oc]], writes=[r_k2[par]])
                mmg([(pDD, projbd[:], qbf_all[oc], True, True)],
                    reads=[r_qbf_all[oc], r_projbd], writes=[rDD])
                act(ddb_all[oc], pDD, AF.Copy,
                    reads=[rDD], writes=[r_ddb_all[oc]])
                mmg([(B0()[0:2, :], onesblk[:], k2b[par], True, True)],
                    reads=[r_k2[par], r_consts], writes=[rB0])
                ve("tensor_scalar", negd[par], B0()[0:2, :], -1.0 / 16.0,
                   None, ALU.mult, reads=[rB0, r_d1[par]],
                   writes=[r_negd[par]])

            def s3(oc):
                par = oc % 2
                pT = vTp if par == 0 else pCbf
                rT = rD0 if par == 0 else rC0
                tposeg([(pT[:, j * 128:(j + 1) * 128],
                         ddb_all[oc][:, j * 128:(j + 1) * 128], identb[:])
                        for j in range(4)],
                       reads=[r_ddb_all[oc], r_consts], writes=[rT])
                veg([("reduce_max",
                      (mxt[par][:, 2 * j:2 * j + 2],
                       pT[:, j * 128:(j + 1) * 128].rearrange(
                           "p (h m) -> p h m", h=2)),
                      dict(axis=AX.X, negate=True)) for j in range(4)],
                    reads=[rT], writes=[r_mxt[par]])

            def s4(oc):
                par = oc % 2
                pDD, rDD = (B1(), rB1) if par == 0 else (D1(), rD1)
                pRB, rRB = (A1(), rA1) if par == 0 else (A0(), rA0)
                mmg([(C1()[0:2, j * 128:(j + 1) * 128], mxt[par][:, 2 * j:2 * j + 2],
                      identb[:], True, j == 3) for j in range(4)],
                    reads=[r_mxt[par], r_consts], writes=[rC1])
                ve("tensor_copy", negmx[par], C1()[0:2, :],
                   reads=[rC1], writes=[r_negmx[par]])
                mmg([(pDD, projbd[:], qbf_all[oc], True, False),
                     (pDD, sel01b[:], negd[par], False, False),
                     (pDD, sel01b[:], negmx[par], False, True)],
                    reads=[r_qbf_all[oc], r_projbd, r_negd[par], r_negmx[par]],
                    writes=[rDD])
                act(qf_sb[par], pDD, AF.Exp, reads=[rDD], writes=[r_qf[par]])
                ve("tensor_scalar", qf_sb[par], qf_sb[par], float(EPS_K), None,
                   ALU.add, reads=[r_qf[par]], writes=[r_qf[par]])
                mmg([(pDD, kvq[s][:, oc * 128:(oc + 1) * 128], qf_sb[par],
                      True, True),
                     (B0()[32:34, :], denpat[s][:, 2 * oc:2 * oc + 2],
                      qf_sb[par], True, True, (0, 32))],
                    reads=[r_qf[par], r_kvq[s], r_denpat[s]],
                    writes=[rDD, rB0])
                act(recip[par], B0()[32:34, :], AF.Ln,
                    reads=[rB0], writes=[r_recip[par]])
                act(d1r[par], recip[par], AF.Exp, scale=f32(-1.0),
                    reads=[r_recip[par]], writes=[r_d1[par]])
                mmg([(pRB, sel01b[:], d1r[par], True, True)],
                    reads=[r_d1[par], r_consts], writes=[rRB])
                ve("tensor_copy", t1f[par], pDD,
                   reads=[rDD], writes=[r_t1[par]])
                ve("tensor_mul", attn_sb[:, oc, :], t1f[par], pRB,
                   reads=[r_t1[par], rRB], writes=[r_attn_c[oc]])
                if debug and tile == 0 and oc == DBG_OC:
                    dma("po", dbg_qf, qf_sb[par], "sdbg", reads=[r_qf[par]])
                    dma("po", dbg_den, denf[par], "sdbg", reads=[r_denf[par]])
                    dma("po", dbg_d1, d1r[par], "sdbg", reads=[r_d1[par]])
                    dma("po", dbg_nmx, negmx[par], "sdbg", reads=[r_negmx[par]])
                    dma("po", dbg_ngd, negd[par], "sdbg", reads=[r_negd[par]])
                    dma("po", dbg_t1, t1f[par], "sdbg", reads=[r_t1[par]])

            for step in range(NCH + 2):
                if step < NCH:
                    s2(step)
                if 1 <= step <= NCH:
                    s3(step - 1)
                if step >= 2:
                    s4(step - 2)
            if debug and tile == 0:
                dma("po", dbg_xn, xn_t[:], "sdbg", reads=r_xn_t)
                dma("po", dbg_attn, attn_sb[:], "sdbg", reads=list(r_attn_c))
            S.mark(f"C{tile}_ao")
            for oc in range(NCH):
                pAO, rAO = [(A0(), rA0), (A1(), rA1),
                            (C0(), rC0)][oc % 3]
                io = take(base + 4 + oc // 2)
                wov = wslot[io][:].rearrange("p (f k j) -> p f k j", f=2, k=NCH)
                if oc == 0:
                    for kc in range(NCH):
                        mmg([(pAO, wov[:, oc % 2, kc, :], attn_sb[:, kc, :],
                              kc == 0, kc == NCH - 1)],
                            reads=[r_attn_c[kc]] + RW(io), writes=[rAO])
                else:
                    mmg([(pAO, wov[:, oc % 2, kc, :], attn_sb[:, kc, :],
                          kc == 0, kc == NCH - 1) for kc in range(NCH)],
                        reads=list(r_attn_c) + RW(io), writes=[rAO])
                ve("tensor_add", res_sb[:, oc, :], x_sb[:, oc, :], pAO,
                   reads=[r_xc[oc], rAO], writes=[r_res])
                act(x2scr[oc % 2], res_sb[:, oc, :], AF.Square,
                    reads=[r_res], writes=[r_x2[oc % 2]])
                mmg([(B0()[0:1, :], onescol_f[:], res_sb[:, oc, :],
                      oc == 0, oc == NCH - 1)],
                    reads=[r_res, r_consts], writes=[rB0])
                mmg([(B0()[32:33, :], onescol_b[:], x2scr[oc % 2],
                      oc == 0, oc == NCH - 1, (0, 32))],
                    reads=[r_x2[oc % 2], r_consts], writes=[rB0])
            ve("tensor_scalar", murow, B0()[0:1, :], 1.0 / D, None, ALU.mult,
               reads=[rB0], writes=[r_rows])
            ve("tensor_mul", musq, murow, murow, reads=[r_rows], writes=[r_rows])
            ve("scalar_tensor_tensor", varrow, B0()[32:33, :], 1.0 / D,
               musq, ALU.mult, ALU.subtract, reads=[rB0, r_rows], writes=[r_rows])
            act(rstd, varrow, AF.Ln, bias=f32(EPS_LN_IN),
                reads=[r_rows], writes=[r_rows])
            act(rstd, rstd, AF.Exp, scale=f32(-0.5), reads=[r_rows], writes=[r_rows])
            mmg([(A0(), onesrow_f[:], rstd, True, True)],
                reads=[r_rows, r_consts], writes=[rA0])
            mmg([(A1(), onesrow_f[:], murow, True, True)],
                reads=[r_rows, r_consts], writes=[rA1])
            for c in range(NCH):
                ve("tensor_sub", tscr[c % 2], res_sb[:, c, :], A1(),
                   reads=[r_res, rA1], writes=[r_tscr[c % 2]])
                ve("tensor_mul", tscr[c % 2], tscr[c % 2], A0(),
                   reads=[r_tscr[c % 2], rA0], writes=[r_tscr[c % 2]])
                ve("tensor_scalar", out1b[:, c, :], tscr[c % 2],
                   g1c[:, c:c + 1], be1c[:, c:c + 1], ALU.mult, ALU.add,
                   reads=[r_tscr[c % 2], r_consts], writes=[r_out1b])
            if debug and tile == 0:
                dma("po", dbg_out1, out1b[:], "sdbg", reads=[r_out1b])
            if c_stage < 2:
                continue
            S.mark(f"C{tile}_w1")
            li = base + NCH
            for g in range(NFC // 2):
                i = take(li)
                wv_ = wslot[i][:].rearrange("p (f k j) -> p f k j", f=2, k=NCH)
                for f_i in range(2):
                    fc = 2 * g + f_i
                    par = fc % 2
                    pH, rH = [(A0(), rA0), (A1(), rA1),
                              (C1(), rC1)][fc % 3]
                    mmg([(pH, wv_[:, f_i, kc, :], out1b[:, kc, :],
                          kc == 0, kc == NCH - 1) for kc in range(NCH)],
                        reads=RW(i) + [r_out1b], writes=[rH])
                    etok = act(escr[par][:], pH, AF.Exp, bias=b1c[:, fc:fc + 1],
                               reads=[rH, r_consts], writes=[r_escr[par]])
                    ve("tensor_scalar", rscr[par][:], pH, b1c[:, fc:fc + 1],
                       0.0, ALU.add, ALU.max,
                       reads=[rH, r_consts], writes=[r_rscr[par]], deps=[etok])
                    ve("scalar_tensor_tensor", h1_sb[:, fc, :], escr[par][:],
                       1.0, rscr[par][:], ALU.min, ALU.add,
                       reads=[r_escr[par], r_rscr[par]], writes=[r_h1])
                li += 1
            if debug and tile == 0:
                dma("po", dbg_h1, h1_sb[:, 0:4, :], "sdbg", reads=[r_h1])
            if c_stage < 3:
                continue
            S.mark(f"C{tile}_w2")
            for g in range(NFC // 2):
                i = take(li)
                wv_ = wslot[i][:].rearrange("p (a d) -> p a d", a=2)
                for k_i in range(2):
                    kc = 2 * g + k_i
                    mmg([(BANK[oc](), wv_[:, k_i, oc * 128:(oc + 1) * 128],
                          h1_sb[:, kc, :], kc == 0, kc == NFC - 1)
                         for oc in range(NCH)],
                        reads=RW(i) + [r_h1], writes=RBANK)
                li += 1
            for oc in range(NCH):
                ve("scalar_tensor_tensor", res_sb[:, oc, :], BANK[oc](),
                   b2eff[:, oc:oc + 1], out1b[:, oc, :], ALU.add, ALU.add,
                   reads=[RBANK[oc], r_b2eff, r_out1b], writes=[r_res])
            S.mark(f"C{tile}_ln2")
            emit_affine_ln(res_sb, r_res, out1b, r_out1b, g2c, be2c, EPS_LN_IN,
                           B0, rB0, A0, rA0, A1, rA1)
            if debug and tile == 0:
                dma("po", dbg_a, out1b[:], "sdbg", reads=[r_out1b])
            if c_stage < 4:
                continue
            for oc in range(NCH):
                ve("tensor_add", res_sb[:, oc, :], x_sb[:, oc, :], out1b[:, oc, :],
                   reads=[r_xc[oc], r_out1b], writes=[r_res])
            S.mark(f"C{tile}_down")
            for p in range(NHC // 2):
                ig = take(base + 40 + 2 * p)
                iu = take(base + 41 + 2 * p)
                wgv = wslot[ig][:].rearrange("p (f k j) -> p f k j", f=2, k=NCH)
                wuv = wslot[iu][:].rearrange("p (f k j) -> p f k j", f=2, k=NCH)
                for f_i in range(2):
                    hc = 2 * p + f_i
                    par = hc % 2
                    pG, rG = [(C0(), rC0), (BANK[6](), rD0),
                              (B0(), rB0)][hc % 3]
                    pU, rU = [(C1(), rC1), (D1(), rD1),
                              (B1(), rB1)][hc % 3]
                    mmg([(pG, wgv[:, f_i, kc, :], xn_t[:, kc, :],
                          kc == 0, kc == NCH - 1) for kc in range(NCH)],
                        reads=RW(ig) + r_xn_t, writes=[rG])
                    act(gscr[par][:], pG, AF.Copy,
                        reads=[rG], writes=[r_gscr[par]])
                    mmg([(pU, wuv[:, f_i, kc, :], xn_t[:, kc, :],
                          kc == 0, kc == NCH - 1) for kc in range(NCH)],
                        reads=RW(iu) + r_xn_t, writes=[rU])
                    act(uscr[par][:], pU, AF.Copy,
                        reads=[rU], writes=[r_uscr[par]])
                    act(gtan[par][:], gscr[par][:], AF.Tanh,
                        scale=f32(0.5 * INV127),
                        reads=[r_gscr[par]], writes=[r_gtan[par]])
                    ve("tensor_mul", gtan[par][:], gscr[par][:], gtan[par][:],
                       reads=[r_gscr[par], r_gtan[par]], writes=[r_gtan[par]])
                    ve("scalar_tensor_tensor", gtan[par][:], gscr[par][:], 1.0,
                       gtan[par][:], ALU.mult, ALU.add,
                       reads=[r_gscr[par], r_gtan[par]], writes=[r_gtan[par]])
                    ve("tensor_mul", hg_sb[:, hc, :], gtan[par][:], uscr[par][:],
                       reads=[r_gtan[par], r_uscr[par]], writes=[r_hg])
                if p == 5 and f_i == 1 and tile + 1 < c_tiles:
                    emit_xtile(tile + 1)
            li = base + 64
            for g in range(NHC // 2):
                i = take(li)
                wv_ = wslot[i][:].rearrange("p (a d) -> p a d", a=2)
                for k_i in range(2):
                    kc = 2 * g + k_i
                    mmg([(BANK[oc](), wv_[:, k_i, oc * 128:(oc + 1) * 128],
                          hg_sb[:, kc, :], kc == 0, kc == NHC - 1)
                         for oc in range(NCH)],
                        reads=RW(i) + [r_hg], writes=RBANK)
                li += 1
            S.mark(f"C{tile}_fin")
            for oc in range(NCH):
                ve("scalar_tensor_tensor", res_sb[:, oc, :], BANK[oc](),
                   float(DOWN_SCALE), res_sb[:, oc, :], ALU.mult, ALU.add,
                   reads=[RBANK[oc]], writes=[r_res])
            t0 = tile * TILE
            dma("ac", outfm[:, t0:t0 + TILE].rearrange("(c p) n -> p c n", p=128),
                res_sb[:], "sout", reads=[r_res])
            if tile + 1 < c_tiles:
                xn_t = xn_sb2[(tile + 1) % 2]
                r_xn_t = r_xn2[(tile + 1) % 2]
                emit_xn(B0, rB0, BANK[6], rD0, D1, rD1)

    S.mark("END")
    import json as _json
    try:
        with open("/tmp/pe_marks.json", "w") as _f:
            _json.dump(S.marks, _f)
    except OSError:
        pass
    final_out = S.counts["sout"]
    final_dbg = S.counts["sdbg"]
    with contextlib.ExitStack() as stack:
        sems = {s_name: stack.enter_context(nc.semaphore(s_name)) for s_name in SEMS}
        block = stack.enter_context(nc.Block())

        def make_section(eng_name, tail=None):
            def section(eng):
                for (waits, fn, sem_n, inc) in S.ops[eng_name]:
                    for (ws, wv) in waits:
                        eng.wait_ge(sems[ws], wv)
                    inst = fn(eng)
                    if inst is not None and hasattr(inst, "then_inc"):
                        inst.then_inc(sems[sem_n], inc)
                if tail is not None:
                    tail(eng)
            return section

        block.sync(make_section("sp"))
        block.tensor(make_section("pe"))
        block.scalar(make_section("ac"))
        block.vector(make_section("ve"))
        def po_tail(e):
            e.wait_ge(sems["sout"], final_out)
            if final_dbg:
                e.wait_ge(sems["sdbg"], final_dbg)
        block.gpsimd(make_section("po", tail=po_tail))

    return nc


_BF = ml_dtypes.bfloat16


def _host_prep(inputs):
    shared = {
        "wqt": np.ascontiguousarray(
            np.asarray(inputs["L0_attn_Wq"]).reshape(NCH, 128, NCH, 128)
            .transpose(2, 1, 0, 3), np.float32),
        "wk": np.ascontiguousarray(inputs["L0_attn_Wk"].reshape(D, D), np.float32),
        "wv": np.ascontiguousarray(inputs["L0_attn_Wv"].reshape(D, D), np.float32),
        "wot": np.ascontiguousarray(
            np.asarray(inputs["L0_attn_Wo"]).reshape(NCH, 128, NCH, 128)
            .transpose(2, 1, 0, 3), np.float32),
        "projT": np.ascontiguousarray(np.asarray(inputs["L0_attn_proj"]).T, np.float32),
        "w1t": np.ascontiguousarray(
            np.asarray(inputs["L0_attn_W1"]).reshape(NCH, 128, NFC, 128)
            .transpose(2, 1, 0, 3), np.float32),
        "w2": np.ascontiguousarray(inputs["L0_attn_W2"], np.float32),
        "b1c": np.ascontiguousarray(
            np.asarray(inputs["L0_attn_b1"]).reshape(NFC, 128).T, np.float32),
        "b2c": np.ascontiguousarray(
            np.asarray(inputs["L0_attn_b2"]).reshape(NCH, 128).T, np.float32),
        "g1c": np.ascontiguousarray(
            np.asarray(inputs["L0_attn_ln1_g"]).reshape(NCH, 128).T, np.float32),
        "be1c": np.ascontiguousarray(
            np.asarray(inputs["L0_attn_ln1_b"]).reshape(NCH, 128).T, np.float32),
        "g2c": np.ascontiguousarray(
            np.asarray(inputs["L0_attn_ln2_g"]).reshape(NCH, 128).T, np.float32),
        "be2c": np.ascontiguousarray(
            np.asarray(inputs["L0_attn_ln2_b"]).reshape(NCH, 128).T, np.float32),
        "wgt": np.ascontiguousarray(
            np.asarray(inputs["L0_ffn_w_gate"]).reshape(NHC, 128, NCH, 128)
            .transpose(0, 3, 2, 1), np.int32),
        "wut": np.ascontiguousarray(
            np.asarray(inputs["L0_ffn_w_up"]).reshape(NHC, 128, NCH, 128)
            .transpose(0, 3, 2, 1), np.int32),
        "wdt": np.ascontiguousarray(np.asarray(inputs["L0_ffn_w_down"]).T, np.int32),
        "identb": np.eye(128, dtype=np.float32).astype(_BF),
        "identf": np.eye(128, dtype=np.float32),
        "onescol_f": np.ones((128, 1), np.float32),
        "onescol_b": np.ones((128, 1), np.float32).astype(_BF),
        "onesrow_f": np.ones((1, 128), np.float32),
        "onesblk": np.block([[np.ones((64, 1)), np.zeros((64, 1))],
                             [np.zeros((64, 1)), np.ones((64, 1))]]).astype(_BF),
        "sel01b": np.block([[np.ones((1, 64)), np.zeros((1, 64))],
                            [np.zeros((1, 64)), np.ones((1, 64))]]).astype(_BF),
        "sel01f": np.block([[np.ones((1, 64)), np.zeros((1, 64))],
                            [np.zeros((1, 64)), np.ones((1, 64))]]).astype(np.float32),
    }
    return shared


def kernel(**inputs):
    x = np.asarray(inputs["L0_attn_x"])
    shared = _host_prep(inputs)
    nc = build_program(debug=DEBUG)
    in_maps = []
    for c in range(NCORES):
        m = dict(shared)
        m["xfm"] = np.ascontiguousarray(x[c * SEQ:(c + 1) * SEQ].reshape(TOK, D).T)
        in_maps.append(m)
    res = run_bass_kernel_spmd(nc, in_maps, core_ids=list(range(NCORES)))
    outs = [res.results[c]["outfm"].T.reshape(SEQ, T, D) for c in range(NCORES)]
    return np.concatenate(outs, axis=0)
